# version 07
# baseline (speedup 1.0000x reference)
import numpy as np
import ml_dtypes
import concourse.bass as bass
import concourse.mybir as mybir
from concourse.bass_utils import run_bass_kernel_spmd
from contextlib import ExitStack

F32 = mybir.dt.float32
BF16 = mybir.dt.bfloat16
AF = mybir.ActivationFunctionType
ALU = mybir.AluOpType
AX = mybir.AxisListType
f32 = np.float32

D = 1024
NH = 16
DH = 64
E = 8
HID = 4096
EPS_LN1 = 1e-5
EPS_LN2 = 1e-3


class Sem:
    def __init__(self, handle):
        self.h = handle
        self.n = 0


class Plan:
    def __init__(self):
        self.ops = {"tensor": [], "vector": [], "scalar": [], "gpsimd": [], "sync": []}

    def em(self, eng, f):
        self.ops[eng].append(f)

    def wait(self, eng, sem, val):
        if val > 0:
            self.ops[eng].append(lambda e, s=sem.h, v=val: e.wait_ge(s, v))


def build(NT=4096, stages="acb", climit=99):
    NTILES = NT // 512
    NG = NT // 128
    NGH = NG // 2
    NTAU = NT // 1024

    nc = bass.Bass("TRN2", target_bir_lowering=False)

    ap_x = nc.dram_tensor("x", [NT, D], F32, kind="ExternalInput").ap()
    w_hbm = {n: nc.dram_tensor(n, [D, D], F32, kind="ExternalInput").ap()
             for n in ["Wq", "Wk", "Wv", "Wq2", "Wk2", "Wv2", "Wo"]}
    b_hbm = {n: nc.dram_tensor(n, [D], F32, kind="ExternalInput").ap()
             for n in ["bq", "bk", "bv", "bq2", "bk2", "bv2", "bo", "gamma", "beta"]}
    ap_Wr = nc.dram_tensor("Wr", [E, D], F32, kind="ExternalInput").ap()
    ap_W1 = nc.dram_tensor("W1", [E, D, HID], F32, kind="ExternalInput").ap()
    ap_b1 = nc.dram_tensor("b1", [E, HID], F32, kind="ExternalInput").ap()
    ap_W2 = nc.dram_tensor("W2", [E, HID, D], F32, kind="ExternalInput").ap()
    ap_b2 = nc.dram_tensor("b2", [E, D], F32, kind="ExternalInput").ap()
    ap_idbf = nc.dram_tensor("id_bf", [128, 128], BF16, kind="ExternalInput").ap()
    ap_idf = nc.dram_tensor("id_f", [128, 128], F32, kind="ExternalInput").ap()
    ap_ones = nc.dram_tensor("ones_f", [1, 128], F32, kind="ExternalInput").ap()
    ap_y = nc.dram_tensor("y", [NT, D], F32, kind="ExternalOutput").ap()

    xn_scr = nc.dram_tensor("xn_scr", [NTILES, 128, 4096], BF16).ap()
    yp_scr = nc.dram_tensor("yp_scr", [NG, 128, D], F32).ap()
    w1_scr = nc.dram_tensor("w1_scr", [E, 8, 128, 4096], BF16).ap()
    w2_scr = nc.dram_tensor("w2_scr", [E, 128, 32 * D], BF16).ap()
    cc_in = nc.dram_tensor("cc_in", [128, 512], F32).ap()
    cc_out = nc.dram_tensor("cc_out", [128, 512], F32).ap()

    P = Plan()
    ctx = ExitStack()

    def sem(name):
        return Sem(ctx.enter_context(nc.semaphore(name)))

    s_pe = sem("s_pe")
    s_dv = sem("s_dv")
    s_ac = sem("s_ac")
    s_cc = sem("s_cc")
    s_ld = sem("s_ld")
    s_ldg = sem("s_ldg")
    s_x = [sem("s_x0"), sem("s_x1")]
    s_xnw = [sem("s_xnw0"), sem("s_xnw1")]
    s_xnr = [sem("s_xnr0"), sem("s_xnr1")]
    s_yp = sem("s_yp")
    s_acc = sem("s_acc")
    s_acc1 = sem("s_acc1")
    s_out = sem("s_out")
    s_out1 = sem("s_out1")
    s_w1 = [sem("s_w1a"), sem("s_w1b")]
    s_w1g = [sem("s_w1ga"), sem("s_w1gb")]
    s_w1w = [sem("s_w1wa"), sem("s_w1wb")]
    s_w2 = sem("s_w2")
    s_w2w = sem("s_w2w")
    s_b2 = [sem("s_b2a"), sem("s_b2b")]
    s_wc = [sem("s_wca"), sem("s_wcb")]
    s_wcw = [sem("s_wcwa"), sem("s_wcwb")]
    DMA_SEMS = [s_ld, s_ldg] + s_x + s_xnw + s_xnr + \
        [s_yp, s_acc, s_acc1, s_out, s_out1] + \
        s_w1 + s_w1g + s_w1w + [s_w2, s_w2w] + s_b2 + s_wc + s_wcw

    def wait_ld(eng):
        P.wait(eng, s_ld, s_ld.n)
        P.wait(eng, s_ldg, s_ldg.n)

    def barrier(engines=("tensor", "vector", "scalar", "sync", "gpsimd")):
        for eng in engines:
            for sm in [s_pe, s_dv, s_ac] + DMA_SEMS:
                P.wait(eng, sm, sm.n)

    def mm(out_ap, lhsT, rhs, start, stop, inc=0):
        def f(t, o=out_ap, l=lhsT, r=rhs, st=start, sp=stop, k=inc):
            i = t.matmul(o, l, r, start=st, stop=sp)
            if k:
                i.then_inc(s_pe.h, k)
        P.em("tensor", f)
        if inc:
            s_pe.n += inc
        return s_pe.n

    def dma(q, out_ap, in_ap, sm):
        def f(e, o=out_ap, i=in_ap, s=sm.h):
            e.dma_start(out=o, in_=i).then_inc(s, 16)
        P.em(q, f)
        sm.n += 16
        return sm.n

    def act(out_ap, in_ap, func, bias=0.0, scale=1.0, accum=None):
        def f(e, o=out_ap, i=in_ap, fu=func, b=bias, s=scale, a=accum):
            kw = dict(bias=b if not isinstance(b, float) else f32(b),
                      scale=s if not isinstance(s, float) else f32(s))
            if a is not None:
                kw["accum_out"] = a
            e.activation(o, i, fu, **kw).then_inc(s_ac.h, 1)
        P.em("scalar", f)
        s_ac.n += 1
        return s_ac.n

    def dve(method, *args, **kw):
        def f(e, m=method, a=args, k=kw):
            getattr(e, m)(*a, **k).then_inc(s_dv.h, 1)
        P.em("vector", f)
        s_dv.n += 1
        return s_dv.n

    id_bf = ctx.enter_context(nc.sbuf_tensor("sb_id_bf", [128, 128], BF16))
    ones_f = ctx.enter_context(nc.sbuf_tensor("sb_ones_f", [1, 128], F32))
    ones_b = ctx.enter_context(nc.sbuf_tensor("sb_ones_b", [1, 128], BF16))
    c_tm = ctx.enter_context(nc.sbuf_tensor("c_tm", [128, NG * 8], F32))
    b_fm = ctx.enter_context(nc.sbuf_tensor("b_fm", [128, 32], F32))
    b1_fm = ctx.enter_context(nc.sbuf_tensor("sb_b1_fm", [128, E * 32], F32))
    stag = ctx.enter_context(nc.sbuf_tensor("stag", [32, 128], F32))

    ac_scope = ExitStack()
    qks = ac_scope.enter_context(nc.sbuf_tensor("qks", [128, NG * 16], F32))
    mrstd = ac_scope.enter_context(nc.sbuf_tensor("mrstd", [128, NG * 2], F32))
    S_sb = ac_scope.enter_context(nc.sbuf_tensor("S_sb", [128, 512], F32))
    S_tmp = ac_scope.enter_context(nc.sbuf_tensor("S_tmp", [64, 1024], F32))
    cstg = ac_scope.enter_context(nc.sbuf_tensor("cstg", [128, 2 * 4096], BF16))

    a_scope = ExitStack()

    def a_sb(name, shape, dtype=F32):
        return a_scope.enter_context(nc.sbuf_tensor(name, shape, dtype))

    def a_ps(name):
        return a_scope.enter_context(nc.psum_tensor(name, [128, 512], F32))

    wq = a_sb("wq", [128, 8 * D], BF16)
    wq2 = a_sb("wq2", [128, 8 * D], BF16)
    wk = a_sb("wk", [128, 8 * D], BF16)
    wk2 = a_sb("wk2", [128, 8 * D], BF16)
    x_sb = a_sb("x_sb", [128, 2 * 4096], F32)
    xn_tm = a_sb("xn_tm", [128, 4096], BF16)
    xn_fm = a_sb("xn_fm", [128, 2 * 4096], BF16)
    q1f = a_sb("q1f", [128, 8 * 512], BF16)
    phq = a_sb("phq", [128, 4096], BF16)
    phk = a_sb("phk", [128, 4096], BF16)
    s1e = a_sb("s1e", [128, 2 * 512], F32)
    s2e = a_sb("s2e", [128, 2 * 512], F32)
    qtmp = a_sb("qtmp", [128, 2 * 1024], F32)
    id_f = a_sb("sb_id_f", [128, 128], F32)
    wr_bf = a_sb("wr_bf", [8, 1024], BF16)
    wrfm = a_sb("wrfm", [128, 64], BF16)
    bq2r = a_sb("bq2r", [1, D], BF16)
    bk2r = a_sb("bk2r", [1, D], BF16)
    absc = a_sb("absc", [128, 48], F32)
    lntmp = a_sb("lntmp", [128, 4], F32)
    r_ex = a_sb("r_ex", [128, 32], F32)
    r_sc = a_sb("r_sc", [128, 16], F32)
    r_t1 = a_sb("r_t1", [128, 8], F32)
    r_t2 = a_sb("r_t2", [128, 8], F32)

    psS0 = a_ps("psS0")
    psS1 = a_ps("psS1")
    psA0, psA1, psA2, psA3 = a_ps("psA0"), a_ps("psA1"), a_ps("psA2"), a_ps("psA3")
    psB0, psB1 = a_ps("psB0"), a_ps("psB1")

    dma("sync", id_bf[:, :], ap_idbf, s_ld)
    dma("sync", id_f[:, :], ap_idf, s_ld)
    dma("sync", ones_f[:, :], ap_ones, s_ld)
    dma("gpsimd", bq2r[:, :], b_hbm["bq2"].rearrange("(o f) -> o f", o=1), s_ldg)
    dma("gpsimd", bk2r[:, :], b_hbm["bk2"].rearrange("(o f) -> o f", o=1), s_ldg)
    dma("gpsimd", ones_b[:, :], ap_ones, s_ldg)
    for w, dst in [("Wq", wq), ("Wq2", wq2), ("Wk", wk), ("Wk2", wk2)]:
        dma("gpsimd", dst[:, :].rearrange("p (kc o) -> p kc o", kc=8),
            w_hbm[w].rearrange("(kc p) o -> p kc o", p=128), s_ldg)
    dma("gpsimd", wr_bf[:, :], ap_Wr, s_ldg)

    stag_free = 0
    for j, bn in enumerate(["bq", "bk", "bv", "bv2"]):
        P.wait("sync", s_pe, stag_free)
        dma("sync", stag[0:8, :], b_hbm[bn].rearrange("(fc p) -> fc p", p=128), s_ld)
        P.wait("tensor", s_ld, s_ld.n)
        P.wait("tensor", s_dv, s_dv.n)
        stag_free = mm(psA0[0:128, 0:8], stag[0:8, :], id_f[0:8, 0:8], True, True, inc=1)
        P.wait("vector", s_pe, s_pe.n)
        dve("tensor_copy", b_fm[:, j * 8:(j + 1) * 8], psA0[0:128, 0:8])
    for e in range(E):
        P.wait("sync", s_pe, stag_free)
        dma("sync", stag[0:32, :], ap_b1[e].rearrange("(hc p) -> hc p", p=128), s_ld)
        P.wait("tensor", s_ld, s_ld.n)
        P.wait("tensor", s_dv, s_dv.n)
        stag_free = mm(psA1[0:128, 0:32], stag[0:32, :], id_f[0:32, 0:32], True, True, inc=1)
        P.wait("vector", s_pe, s_pe.n)
        dve("tensor_copy", b1_fm[:, e * 32:(e + 1) * 32], psA1[0:128, 0:32])
    wait_ld("tensor")
    for dc in range(8):
        mm(psA2[0:128, dc * 8:(dc + 1) * 8], wr_bf[0:8, dc * 128:(dc + 1) * 128],
           id_bf[0:8, 0:8], True, True, inc=(1 if dc == 7 else 0))
    P.wait("vector", s_pe, s_pe.n)
    dve("tensor_copy", wrfm[:, :], psA2[0:128, 0:64])
    v_pre_dv = s_dv.n

    a_xv = {}
    a_pre_dv = {}

    def a_load(t):
        if t >= NTILES:
            return
        sl_ = t % 2
        P.wait("sync", s_dv, xload_done.get(t - 2, 0))
        a_xv[t] = dma("sync",
                      x_sb[:, sl_ * 4096:(sl_ + 1) * 4096]
                      .rearrange("p (g f) -> p g f", g=4),
                      ap_x.rearrange("(t g p) f -> t p g f", p=128, g=4)[t], s_x[sl_])

    def a_pre(t, xnT_prev):
        if t >= NTILES:
            return
        sl_ = t % 2
        P.wait("vector", s_x[sl_], a_xv[t])
        for g in range(4):
            xg = x_sb[:, sl_ * 4096 + g * 1024: sl_ * 4096 + (g + 1) * 1024]
            dve("bn_stats", absc[:, g * 12:g * 12 + 6], xg[:, 0:512])
            dve("bn_stats", absc[:, g * 12 + 6:g * 12 + 12], xg[:, 512:1024])
        P.wait("vector", s_dv, s_dv.n)
        for g in range(4):
            G = t * 4 + g
            dve("bn_aggr", mrstd[:, G * 2:G * 2 + 2], absc[:, g * 12:g * 12 + 12])
        vagg = s_dv.n
        P.wait("scalar", s_dv, vagg)
        var_v = mrstd[:, (t * 4) * 2 + 1:(t * 4 + 4) * 2:2]
        act(lntmp[:, 0:4], var_v, AF.Ln, bias=EPS_LN1)
        P.wait("scalar", s_ac, s_ac.n)
        act(var_v, lntmp[:, 0:4], AF.Exp, scale=-0.5)
        P.wait("vector", s_ac, s_ac.n)
        P.wait("vector", s_pe, xnT_prev)
        for g in range(4):
            G = t * 4 + g
            dve("tensor_scalar", xn_tm[:, g * 1024:(g + 1) * 1024],
                x_sb[:, sl_ * 4096 + g * 1024: sl_ * 4096 + (g + 1) * 1024],
                mrstd[:, G * 2:G * 2 + 1], mrstd[:, G * 2 + 1:G * 2 + 2],
                ALU.subtract, ALU.mult)
        xload_done[t] = s_dv.n
        a_pre_dv[t] = s_dv.n

    w1ap_pre = ap_W1.rearrange("e (dc p) h -> e p dc h", p=128)
    conv_jobs = [(e, sl_) for e in range(E) for sl_ in range(8)]
    conv_state = {"rd": 0, "wr": 0, "rd_v": {}, "wr_v": {0: 0, 1: 0}}

    def conv_read():
        i = conv_state["rd"]
        if i >= len(conv_jobs):
            return
        e_, s_ = conv_jobs[i]
        slot = i % 2
        P.wait("gpsimd", s_wcw[slot], conv_state["wr_v"][slot])
        conv_state["rd_v"][i] = dma(
            "gpsimd", cstg[:, slot * 4096:(slot + 1) * 4096]
            .rearrange("p (dc h) -> p dc h", dc=8),
            w1ap_pre[e_][:, :, s_ * 512:(s_ + 1) * 512], s_wc[slot])
        conv_state["rd"] = i + 1

    def conv_write():
        i = conv_state["wr"]
        if i >= len(conv_jobs) or i >= conv_state["rd"]:
            return
        e_, s_ = conv_jobs[i]
        slot = i % 2
        P.wait("sync", s_wc[slot], conv_state["rd_v"][i])
        conv_state["wr_v"][slot] = dma(
            "sync", w1_scr[e_, s_].rearrange("p (dc h) -> p dc h", dc=8),
            cstg[:, slot * 4096:(slot + 1) * 4096]
            .rearrange("p (dc h) -> p dc h", dc=8), s_wcw[slot])
        conv_state["wr"] = i + 1

    def conv_step(n):
        for _ in range(n):
            conv_write()
            conv_read()

    conv_read()
    conv_read()

    xload_done = {}
    xnfm_free = {}
    xnfm_dma = {}
    q2_evac = {}
    bank_act = {}
    bank_dve = {}
    router_read = 0
    xnT_done = 0
    s_started = [False, False]
    phq_free = 0

    xnT_state = {"done": 0, "v_xnfm": {}}

    def a_xnT(t):
        if t >= NTILES:
            return
        sl_ = t % 2
        P.wait("tensor", s_dv, a_pre_dv[t])
        P.wait("tensor", s_pe, xnfm_free.get(t - 2, 0))
        P.wait("vector", s_xnw[sl_], xnfm_dma.get(t - 2, 0))
        for g in range(4):
            bank = psA0 if g % 2 == 0 else psA1
            P.wait("tensor", s_dv, bank_dve.get(("T", g % 2), 0))
            for fc in range(8):
                mm(bank[:, (fc % 4) * 128:(fc % 4) * 128 + 128],
                   xn_tm[:, g * 1024 + fc * 128: g * 1024 + (fc + 1) * 128],
                   id_bf[:, :], True, True, inc=(1 if fc % 4 == 3 else 0))
                if fc == 3 or fc == 7:
                    lo = 0 if fc == 3 else 4
                    P.wait("vector", s_pe, s_pe.n)
                    dve("tensor_copy",
                        xn_fm[:, sl_ * 4096:(sl_ + 1) * 4096]
                        .rearrange("p (fc tl) -> p fc tl", fc=8)
                        [:, lo:lo + 4, g * 128:(g + 1) * 128],
                        bank[:, 0:512].rearrange("p (s q) -> p s q", s=4))
                    bank_dve[("T", g % 2)] = s_dv.n
                    if fc == 3:
                        P.wait("tensor", s_dv, s_dv.n)
        xnT_state["v_xnfm"][t] = s_dv.n
        xnT_state["done"] = s_pe.n
        P.wait("gpsimd", s_dv, s_dv.n)
        xnfm_dma[t] = dma("gpsimd", xn_scr[t],
                          xn_fm[:, sl_ * 4096:(sl_ + 1) * 4096], s_xnw[sl_])

    a_load(0)
    a_pre(0, 0)
    a_xnT(0)
    a_load(1)
    for t in range(NTILES):
        sl = t % 2
        v_xnfm = xnT_state["v_xnfm"][t]

        a_pre(t + 1, xnT_state["done"])
        a_xnT(t + 1)
        a_load(t + 2)

        conv_step(64 // (2 * NTILES) if NTILES else 0)

        if t == 0:
            wait_ld("tensor")
        for kind in ("q", "k"):
            w1_, w2_, bcol, brow_ = ((wq, wq2, 0, bq2r) if kind == "q"
                                     else (wk, wk2, 1, bk2r))
            ph = phq if kind == "q" else phk
            P.wait("tensor", s_dv, max([q2_evac.get((a, b), 0)
                                        for a in (0, 1) for b in (0, 1)] + [v_xnfm]))
            for fc in range(8):
                bank = psA2 if fc % 2 == 0 else psA3
                P.wait("tensor", s_ac, bank_act.get(("P1", fc % 2), 0))
                for dc in range(8):
                    mm(bank[:, 0:512],
                       w1_[:, dc * D + fc * 128: dc * D + (fc + 1) * 128],
                       xn_fm[:, sl * 4096 + dc * 512: sl * 4096 + (dc + 1) * 512],
                       dc == 0, dc == 7, inc=(1 if dc == 7 else 0))
                P.wait("scalar", s_pe, s_pe.n)
                act(q1f[:, fc * 512:(fc + 1) * 512], bank[:, 0:512],
                    AF.Identity, bias=b_fm[:, bcol * 8 + fc: bcol * 8 + fc + 1])
                bank_act[("P1", fc % 2)] = s_ac.n
            v_q1 = s_ac.n
            P.wait("tensor", s_ac, v_q1)
            for g in range(4):
                pair = (psB0, psB1) if g % 2 == 0 else (psA2, psA3)
                for hi in range(2):
                    bank = pair[hi]
                    P.wait("tensor", s_dv, max(q2_evac.get((g % 2, hi), 0),
                                               router_read))
                    for fc in range(8):
                        mm(bank[:, 0:512],
                           q1f[:, fc * 512 + g * 128: fc * 512 + (g + 1) * 128],
                           w2_[:, fc * D + hi * 512: fc * D + (hi + 1) * 512],
                           fc == 0, False)
                    mm(bank[:, 0:512], ones_b[:, :],
                       brow_[:, hi * 512:(hi + 1) * 512], False, True, inc=1)
                    vpe = s_pe.n
                    P.wait("scalar", s_pe, vpe)
                    P.wait("scalar", s_dv, q2_evac.get(("s2", hi), 0))
                    act(s1e[:, hi * 512:(hi + 1) * 512],
                        bank[:, 0:512], AF.Relu, scale=-1.0)
                    P.wait("scalar", s_ac, s_ac.n)
                    act(s2e[:, hi * 512:(hi + 1) * 512],
                        s1e[:, hi * 512:(hi + 1) * 512], AF.Exp, scale=-1.0)
                    bank_act[("E", hi)] = s_ac.n
                    P.wait("vector", s_ac, s_ac.n)
                    P.wait("vector", s_pe, vpe)
                    dve("scalar_tensor_tensor",
                        ph[:, g * 1024 + hi * 512: g * 1024 + (hi + 1) * 512],
                        bank[:, 0:512], 0.0,
                        s2e[:, hi * 512:(hi + 1) * 512], ALU.max, ALU.add)
                    q2_evac[(g % 2, hi)] = s_dv.n
                    q2_evac[("s2", hi)] = s_dv.n
        v_phik = s_dv.n

        P.wait("vector", s_dv, v_phik)
        qred = {}
        for g in range(4):
            G = t * 4 + g
            qs = g % 2
            P.wait("vector", s_dv, qred.get(qs, 0))
            dve("tensor_tensor", qtmp[:, qs * 1024:(qs + 1) * 1024],
                phq[:, g * 1024:(g + 1) * 1024],
                phk[:, g * 1024:(g + 1) * 1024], ALU.mult)
            P.wait("vector", s_dv, s_dv.n)
            dve("tensor_reduce", qks[:, G * 16:(G + 1) * 16],
                qtmp[:, qs * 1024:(qs + 1) * 1024]
                .rearrange("p (h k) -> p h k", k=64), AX.X, ALU.add)
            qred[qs] = s_dv.n
        P.wait("tensor", s_dv, v_phik)
        for g in range(4):
            for h in range(NH):
                bankS = psS0 if h % 2 == 0 else psS1
                c0 = (h // 2) * 64
                first = not s_started[h % 2]
                s_started[h % 2] = True
                last = (t == NTILES - 1 and g == 3 and h >= NH - 2)
                mm(bankS[0:64, c0:c0 + 64],
                   phk[:, g * 1024 + h * 64: g * 1024 + (h + 1) * 64],
                   phq[:, g * 1024 + h * 64: g * 1024 + (h + 1) * 64],
                   first, last, inc=(1 if (g == 3 and h == NH - 1) else 0))
        phq_free = s_pe.n

        P.wait("tensor", s_dv, max([q2_evac.get((a, b), 0)
                                    for a in (0, 1) for b in (0, 1)] + [router_read]))
        for g in range(4):
            for dc in range(8):
                mm(psB0[:, g * 8:(g + 1) * 8],
                   xn_fm[:, sl * 4096 + dc * 512 + g * 128:
                         sl * 4096 + dc * 512 + (g + 1) * 128],
                   wrfm[:, dc * 8:(dc + 1) * 8],
                   dc == 0, dc == 7, inc=(1 if (g == 3 and dc == 7) else 0))
        xnfm_free[t] = s_pe.n
        v_rpe = s_pe.n
        for g in range(4):
            G = t * 4 + g
            P.wait("vector", s_pe, v_rpe)
            dve("reduce_max", r_sc[:, g:g + 1], psB0[:, g * 8:(g + 1) * 8], AX.X,
                negate=True)
            P.wait("scalar", s_dv, s_dv.n)
            P.wait("scalar", s_pe, v_rpe)
            act(r_ex[:, g * 8:(g + 1) * 8], psB0[:, g * 8:(g + 1) * 8], AF.Exp,
                bias=r_sc[:, g:g + 1], accum=r_sc[:, 4 + g:5 + g])
            P.wait("vector", s_ac, s_ac.n)
            dve("reduce_max", r_sc[:, 8 + g:9 + g], r_ex[:, g * 8:(g + 1) * 8], AX.X)
            P.wait("vector", s_dv, s_dv.n)
            dve("tensor_scalar", r_t1[:, :], r_ex[:, g * 8:(g + 1) * 8],
                r_sc[:, 8 + g:9 + g], None, ALU.is_ge)
            P.wait("vector", s_dv, s_dv.n)
            dve("scalar_tensor_tensor", r_t2[:, :], r_t1[:, :], -4.0,
                r_ex[:, g * 8:(g + 1) * 8], ALU.mult, ALU.add)
            P.wait("vector", s_dv, s_dv.n)
            dve("reduce_max", r_sc[:, 12 + g:13 + g], r_t2[:, :], AX.X)
            P.wait("vector", s_dv, s_dv.n)
            dve("tensor_scalar", r_t1[:, :], r_ex[:, g * 8:(g + 1) * 8],
                r_sc[:, 12 + g:13 + g], None, ALU.is_ge)
            dve("reciprocal", r_sc[:, 4 + g:5 + g], r_sc[:, 4 + g:5 + g])
            P.wait("vector", s_dv, s_dv.n)
            dve("tensor_tensor", r_t2[:, :], r_ex[:, g * 8:(g + 1) * 8],
                r_t1[:, :], ALU.mult)
            P.wait("vector", s_dv, s_dv.n)
            dve("tensor_scalar", c_tm[:, G * 8:(G + 1) * 8], r_t2[:, :],
                r_sc[:, 4 + g:5 + g], None, ALU.mult)
        router_read = s_dv.n

    P.wait("vector", s_pe, s_pe.n)
    dve("tensor_copy", S_tmp[:, 0:512], psS0[0:64, 0:512])
    dve("tensor_copy", S_tmp[:, 512:1024], psS1[0:64, 0:512])
    P.wait("sync", s_dv, s_dv.n)
    dma("sync", cc_in[0:64, :], S_tmp[:, 0:512], s_ld)
    dma("sync", cc_in[64:128, :], S_tmp[:, 512:1024], s_ld)
    P.wait("gpsimd", s_ld, s_ld.n)

    def f_cc(e):
        e.collective_compute(
            "AllReduce", ALU.add,
            replica_groups=[[0, 1], [2, 3], [4, 5], [6, 7]],
            ins=[cc_in], outs=[cc_out]).then_inc(s_cc.h, 1)
    P.em("gpsimd", f_cc)
    s_cc.n += 1
    P.wait("gpsimd", s_cc, 1)
    v_sfull = dma("gpsimd", S_sb[:, :], cc_out, s_ldg)

    a_scope.close()

    if stages == "a":
        barrier(("sync",))
        v_fin = dma("sync", ap_y, ap_x, s_out)
        P.wait("sync", s_out, v_fin)
        with nc.Block() as block:
            block.tensor(lambda e: [f(e) for f in P.ops["tensor"]])
            block.vector(lambda e: [f(e) for f in P.ops["vector"]])
            block.scalar(lambda e: [f(e) for f in P.ops["scalar"]])
            block.gpsimd(lambda e: [f(e) for f in P.ops["gpsimd"]])
            block.sync(lambda e: [f(e) for f in P.ops["sync"]])
        ac_scope.close()
        ctx.close()
        return nc

    c_scope = ExitStack()

    def c_sb(name, shape, dtype=F32):
        return c_scope.enter_context(nc.sbuf_tensor(name, shape, dtype))

    def c_ps(name):
        return c_scope.enter_context(nc.psum_tensor(name, [128, 512], F32))

    wo = c_sb("wo", [128, 8 * D], BF16)
    wv = c_sb("wv", [128, 8 * D], BF16)
    wv2 = c_sb("wv2", [128, 8 * D], BF16)
    xc = c_sb("xc", [128, 4096], F32)
    xnf_c = c_sb("xnf_c", [128, 4096], BF16)
    xn32 = c_sb("xn32", [128, 4096], F32)
    v1f = c_sb("v1f", [128, 8 * 512], BF16)
    vf = c_sb("vf", [128, 8 * 512], BF16)
    atf = c_sb("atf", [128, 4096], BF16)
    y_c = c_sb("y_c", [128, 4096], F32)
    z_c = c_sb("z_c", [128, 4096], F32)
    gbc = c_sb("gbc", [128, D], F32)
    bbc = c_sb("bbc", [128, D], F32)
    grow = c_sb("grow", [1, D], F32)
    brow = c_sb("brow", [1, D], F32)
    borow = c_sb("borow", [1, D], BF16)
    S_blk = c_sb("S_blk", [128, 1024], BF16)
    qks_inv = c_sb("qks_inv", [128, NG * 16], F32)
    cst = c_sb("cst", [128, 64], F32)
    clnt = c_sb("clnt", [128, 4], F32)

    psV0, psV1 = c_ps("psV0"), c_ps("psV1")
    psO0, psO1 = c_ps("psO0"), c_ps("psO1")
    psT0, psT1 = c_ps("psT0"), c_ps("psT1")
    psW0, psW1 = c_ps("psW0"), c_ps("psW1")

    barrier()

    for w, dst in [("Wv", wv), ("Wv2", wv2), ("Wo", wo)]:
        dma("gpsimd", dst[:, :].rearrange("p (kc o) -> p kc o", kc=8),
            w_hbm[w].rearrange("(kc p) o -> p kc o", p=128), s_ldg)
    v_cw = s_ldg.n
    dma("sync", grow[:, :], b_hbm["gamma"].rearrange("(o f) -> o f", o=1), s_ld)
    dma("sync", brow[:, :], b_hbm["beta"].rearrange("(o f) -> o f", o=1), s_ld)
    dma("gpsimd", borow[:, :], b_hbm["bo"].rearrange("(o f) -> o f", o=1), s_ldg)
    P.wait("tensor", s_ld, s_ld.n)
    for (row, dst) in [(grow, gbc), (brow, bbc)]:
        for hi in range(2):
            P.wait("tensor", s_dv, s_dv.n)
            mm(psW0[:, 0:512], ones_f[:, :], row[:, hi * 512:(hi + 1) * 512],
               True, True, inc=1)
            P.wait("vector", s_pe, s_pe.n)
            dve("tensor_copy", dst[:, hi * 512:(hi + 1) * 512], psW0[:, 0:512])
    P.wait("vector", s_ldg, v_sfull)
    dve("memset", S_blk[:, :], 0.0)
    P.wait("vector", s_dv, s_dv.n)
    dve("tensor_copy",
        S_blk[0:64, :].rearrange("p (j c) -> p j c", c=128)[:, :, 0:64],
        S_sb[0:64, :].rearrange("p (j c) -> p j c", c=64))
    dve("tensor_copy",
        S_blk[64:128, :].rearrange("p (j c) -> p j c", c=128)[:, :, 64:128],
        S_sb[64:128, :].rearrange("p (j c) -> p j c", c=64))
    dve("reciprocal", qks_inv[:, :], qks[:, :])
    v_cpre = s_dv.n

    c_xused = {}
    c_ypdma = {}
    cb_act = {}
    cb_dve = {}
    xnf_used = 0

    for t in range(NTILES):
        sl = 0
        P.wait("sync", s_dv, c_xused.get(t - 1, 0))
        v_cx = dma("sync",
                   xc[:, :].rearrange("p (g f) -> p g f", g=4),
                   ap_x.rearrange("(t g p) f -> t p g f", p=128, g=4)[t], s_x[0])
        P.wait("sync", s_pe, xnf_used)
        v_cxn = dma("sync", xnf_c[:, :], xn_scr[t], s_xnr[0])

        P.wait("vector", s_x[0], v_cx)
        P.wait("vector", s_yp, c_ypdma.get(t - 1, 0))
        if t == 0:
            P.wait("vector", s_ld, s_ld.n)
        for g in range(4):
            G = t * 4 + g
            dve("tensor_scalar", xn32[:, g * 1024:(g + 1) * 1024],
                xc[:, g * 1024:(g + 1) * 1024],
                mrstd[:, G * 2:G * 2 + 1], mrstd[:, G * 2 + 1:G * 2 + 2],
                ALU.subtract, ALU.mult)
            dve("tensor_tensor", z_c[:, g * 1024:(g + 1) * 1024],
                xc[:, g * 1024:(g + 1) * 1024],
                bbc[:, :], ALU.add)
        v_xn32 = s_dv.n
        c_xused[t] = s_dv.n

        conv_step(64 // (2 * NTILES) if NTILES else 0)

        P.wait("tensor", s_xnr[0], v_cxn)
        P.wait("tensor", s_ldg, s_ldg.n)
        for fc in range(8):
            bank = psV0 if fc % 2 == 0 else psV1
            P.wait("tensor", s_ac, cb_act.get(("V1", fc % 2), 0))
            for dc in range(8):
                mm(bank[:, 0:512],
                   wv[:, dc * D + fc * 128: dc * D + (fc + 1) * 128],
                   xnf_c[:, dc * 512:(dc + 1) * 512],
                   dc == 0, dc == 7, inc=(1 if dc == 7 else 0))
            P.wait("scalar", s_pe, s_pe.n)
            act(v1f[:, fc * 512:(fc + 1) * 512], bank[:, 0:512],
                AF.Identity, bias=b_fm[:, 16 + fc:17 + fc])
            cb_act[("V1", fc % 2)] = s_ac.n
        v_v1 = s_ac.n
        xnf_used = s_pe.n
        P.wait("tensor", s_ac, v_v1)
        for fc in range(8):
            bank = psV0 if fc % 2 == 0 else psV1
            P.wait("tensor", s_ac, cb_act.get(("V2", fc % 2), 0))
            for dc in range(8):
                mm(bank[:, 0:512],
                   wv2[:, dc * D + fc * 128: dc * D + (fc + 1) * 128],
                   v1f[:, dc * 512:(dc + 1) * 512],
                   dc == 0, dc == 7, inc=(1 if dc == 7 else 0))
            P.wait("scalar", s_pe, s_pe.n)
            act(vf[:, fc * 512:(fc + 1) * 512], bank[:, 0:512],
                AF.Identity, bias=b_fm[:, 24 + fc:25 + fc])
            cb_act[("V2", fc % 2)] = s_ac.n
        v_v2 = s_ac.n
        if climit < 1:
            continue

        P.wait("tensor", s_ac, v_v2)
        if t == 0:
            P.wait("tensor", s_dv, v_cpre)
        for g in range(4):
            pair = (psO0, psO1) if g % 2 == 0 else (psT0, psT1)
            G = t * 4 + g
            P.wait("tensor", s_dv, cb_dve.get(("O", g % 2), 0))
            for j in range(8):
                bank = pair[j // 4]
                mm(bank[:, (j % 4) * 128:(j % 4) * 128 + 128],
                   vf[:, j * 512 + g * 128: j * 512 + (g + 1) * 128],
                   S_blk[:, j * 128:(j + 1) * 128],
                   True, True, inc=(1 if j == 7 else 0))
            P.wait("vector", s_pe, s_pe.n)
            for h in range(NH):
                j = h // 2
                bank = pair[j // 4]
                dve("tensor_scalar",
                    v1f[:, g * 1024 + h * 64: g * 1024 + (h + 1) * 64],
                    bank[:, (j % 4) * 128 + (h % 2) * 64:
                         (j % 4) * 128 + (h % 2) * 64 + 64],
                    qks_inv[:, G * 16 + h:G * 16 + h + 1], None, ALU.mult)
            cb_dve[("O", g % 2)] = s_dv.n
        v_atm = s_dv.n
        if climit < 2:
            continue

        P.wait("tensor", s_dv, v_atm)
        for g in range(4):
            bank = psT0 if g % 2 == 0 else psT1
            P.wait("tensor", s_dv, cb_dve.get(("AT", g % 2), 0))
            for fc in range(8):
                mm(bank[:, (fc % 4) * 128:(fc % 4) * 128 + 128],
                   v1f[:, g * 1024 + fc * 128: g * 1024 + (fc + 1) * 128],
                   id_bf[:, :], True, True, inc=(1 if fc % 4 == 3 else 0))
                if fc == 3 or fc == 7:
                    lo = 0 if fc == 3 else 4
                    P.wait("vector", s_pe, s_pe.n)
                    dve("tensor_copy",
                        atf[:, :].rearrange("p (fc tl) -> p fc tl", fc=8)
                        [:, lo:lo + 4, g * 128:(g + 1) * 128],
                        bank[:, 0:512].rearrange("p (s q) -> p s q", s=4))
                    cb_dve[("AT", g % 2)] = s_dv.n
                    if fc == 3:
                        P.wait("tensor", s_dv, s_dv.n)
        v_atf = s_dv.n
        if climit < 3:
            continue

        if t == 0:
            P.wait("tensor", s_ldg, s_ldg.n)
        P.wait("tensor", s_dv, v_atf)
        for g in range(4):
            for hi in range(2):
                bank = psW0 if hi == 0 else psW1
                P.wait("tensor", s_dv, cb_dve.get(("W", hi), 0))
                for fc in range(8):
                    mm(bank[:, 0:512],
                       atf[:, fc * 512 + g * 128: fc * 512 + (g + 1) * 128],
                       wo[:, fc * D + hi * 512: fc * D + (hi + 1) * 512],
                       fc == 0, False)
                mm(bank[:, 0:512], ones_b[:, :],
                   borow[:, hi * 512:(hi + 1) * 512], False, True, inc=1)
                P.wait("vector", s_pe, s_pe.n)
                P.wait("vector", s_dv, v_xn32)
                dve("tensor_tensor",
                    y_c[:, g * 1024 + hi * 512: g * 1024 + (hi + 1) * 512],
                    bank[:, 0:512],
                    xn32[:, g * 1024 + hi * 512: g * 1024 + (hi + 1) * 512],
                    ALU.add)
                cb_dve[("W", hi)] = s_dv.n
        if climit < 4:
            continue
        P.wait("vector", s_dv, s_dv.n)
        for g in range(4):
            yg = y_c[:, g * 1024:(g + 1) * 1024]
            dve("bn_stats", cst[:, g * 12:g * 12 + 6], yg[:, 0:512])
            dve("bn_stats", cst[:, g * 12 + 6:g * 12 + 12], yg[:, 512:1024])
        P.wait("vector", s_dv, s_dv.n)
        for g in range(4):
            dve("bn_aggr", cst[:, 48 + g * 2:48 + g * 2 + 2],
                cst[:, g * 12:g * 12 + 12])
        P.wait("scalar", s_dv, s_dv.n)
        var_v = cst[:, 49:57:2]
        act(clnt[:, 0:4], var_v, AF.Ln, bias=EPS_LN2)
        P.wait("scalar", s_ac, s_ac.n)
        act(var_v, clnt[:, 0:4], AF.Exp, scale=-0.5)
        P.wait("vector", s_ac, s_ac.n)
        for g in range(4):
            yg = y_c[:, g * 1024:(g + 1) * 1024]
            xg = xn32[:, g * 1024:(g + 1) * 1024]
            dve("tensor_scalar", xg, yg, cst[:, 48 + g * 2:48 + g * 2 + 1],
                cst[:, 49 + g * 2:50 + g * 2], ALU.subtract, ALU.mult)
            P.wait("vector", s_dv, s_dv.n)
            dve("tensor_tensor", yg, xg, gbc[:, :], ALU.mult)
            P.wait("vector", s_dv, s_dv.n)
            dve("tensor_tensor", xg, yg,
                z_c[:, g * 1024:(g + 1) * 1024], ALU.add)
        P.wait("gpsimd", s_dv, s_dv.n)
        c_ypdma[t] = dma("gpsimd",
                         yp_scr.rearrange("(t g) p f -> t p g f", g=4)[t],
                         xn32[:, :].rearrange("p (g f) -> p g f", g=4), s_yp)

    conv_step(4)
    c_scope.close()
    ac_scope.close()

    if stages == "ac":
        barrier(("sync",))
        if climit >= 4:
            v_fin = dma("sync", ap_y.rearrange("(G p) f -> G p f", p=128), yp_scr, s_out)
        else:
            v_fin = dma("sync", ap_y, ap_x, s_out)
        P.wait("sync", s_out, v_fin)
        with nc.Block() as block:
            block.tensor(lambda e: [f(e) for f in P.ops["tensor"]])
            block.vector(lambda e: [f(e) for f in P.ops["vector"]])
            block.scalar(lambda e: [f(e) for f in P.ops["scalar"]])
            block.gpsimd(lambda e: [f(e) for f in P.ops["gpsimd"]])
            block.sync(lambda e: [f(e) for f in P.ops["sync"]])
        ctx.close()
        return nc

    b_scope = ExitStack()

    def b_sb(name, shape, dtype=F32):
        return b_scope.enter_context(nc.sbuf_tensor(name, shape, dtype))

    w2e = b_sb("w2e", [128, 32 * D], BF16)
    acc = b_sb("acc", [128, NGH * D], F32)
    w1s = b_sb("w1s", [128, 2 * 4096], BF16)
    hbuf = b_sb("hbuf", [128, 32 * 512], BF16)
    xnb = b_sb("xnb", [128, 2 * 4096], BF16)
    b2r = b_sb("b2r", [1, 2 * D], BF16)

    psH = [b_scope.enter_context(nc.psum_tensor(f"psH{i}", [128, 512], F32))
           for i in range(4)]
    psY = [b_scope.enter_context(nc.psum_tensor(f"psY{i}", [128, 512], F32))
           for i in range(4)]

    barrier()

    w1ap = ap_W1.rearrange("e (dc p) h -> e p dc h", p=128)
    w2ap = ap_W2.rearrange("e (hc p) o -> e p hc o", p=128)

    w1s_free = {0: 0, 1: 0}
    w1s_scrw = {0: 0, 1: 0}
    xnb_free = {0: 0, 1: 0}
    b2_free = {0: 0, 1: 0}
    w2_free = 0
    w2_scrw = 0
    psH_act = [0] * 4
    psY_dve = [0, 0]
    hused = 0
    acc_out = 0
    acc_evac = 0

    def w1slab_src(e, s):
        return w1_scr[e, s].rearrange("p (dc h) -> p dc h", dc=8)

    y_tau = ap_y.rearrange("(hf ta g p) f -> hf ta p g f", p=128, g=4, ta=NTAU)
    yp_tau = yp_scr.rearrange("(hf ta g) p f -> hf ta p g f", g=4, ta=NTAU)
    out_sems = [s_out, s_out1]
    acc_sems = [s_acc, s_acc1]
    out_vals = {}
    acc_vals = {}
    for hf in range(2):
        for ta in range(NTAU):
            if hf == 1:
                P.wait("sync", out_sems[ta % 2], out_vals[(0, ta)])
            acc_vals[(hf, ta)] = dma(
                "sync",
                acc[:, :].rearrange("p (ta g f) -> p ta g f", ta=NTAU, g=4)[:, ta],
                yp_tau[hf, ta], acc_sems[ta % 2])

        for e in range(E):
            bslot = e % 2
            P.wait("sync", s_pe, b2_free[bslot])
            v_b2 = dma("gpsimd", b2r[:, bslot * D:(bslot + 1) * D],
                       ap_b2[e].rearrange("(o f) -> o f", o=1), s_b2[bslot])

            w2chunk_vals = []

            for tau in range(NTAU):
                t_abs = hf * NTAU + tau
                xslot = t_abs % 2
                P.wait("sync", s_pe, xnb_free[xslot])
                v_xnb = dma("sync", xnb[:, xslot * 4096:(xslot + 1) * 4096],
                            xn_scr[t_abs], s_xnr[xslot])

                first_touch = (hf == 0 and tau == 0)
                slab_v = {}
                for s in (0, 1):
                    sw = s_w1[s]
                    P.wait("sync", s_pe, w1s_free[s])
                    slab_v[s] = (sw, dma("sync", w1s[:, s * 4096:(s + 1) * 4096]
                                         .rearrange("p (dc h) -> p dc h", dc=8),
                                         w1slab_src(e, s), sw))
                    if tau == 0:
                        ci = len(w2chunk_vals)
                        if ci < 8:
                            if ci == 0:
                                P.wait("gpsimd", s_pe, w2_free)
                                P.wait("gpsimd", s_w2w, w2_scrw)
                            src = (w2ap[e][:, ci * 4:(ci + 1) * 4, :] if hf == 0 else
                                   w2_scr[e][:, ci * 4 * D:(ci + 1) * 4 * D]
                                   .rearrange("p (hc o) -> p hc o", hc=4))
                            w2chunk_vals.append(
                                dma("gpsimd",
                                    w2e[:, ci * 4 * D:(ci + 1) * 4 * D]
                                    .rearrange("p (hc o) -> p hc o", hc=4),
                                    src, s_w2))
                P.wait("tensor", s_xnr[xslot], v_xnb)
                for s in range(8):
                    slot = s % 2
                    P.wait("tensor", slab_v[s][0], slab_v[s][1])
                    for hcl in range(4):
                        hc = s * 4 + hcl
                        bank = psH[hc % 4]
                        P.wait("tensor", s_ac, psH_act[hc % 4])
                        for dc in range(8):
                            mm(bank[:, 0:512],
                               w1s[:, slot * 4096 + dc * 512 + hcl * 128:
                                   slot * 4096 + dc * 512 + (hcl + 1) * 128],
                               xnb[:, xslot * 4096 + dc * 512:
                                   xslot * 4096 + (dc + 1) * 512],
                               dc == 0, dc == 7, inc=(1 if dc == 7 else 0))
                        if hc == 0:
                            P.wait("scalar", s_pe, hused)
                        P.wait("scalar", s_pe, s_pe.n)
                        act(hbuf[:, hc * 512:(hc + 1) * 512], bank[:, 0:512],
                            AF.Gelu, bias=b1_fm[:, e * 32 + hc:e * 32 + hc + 1])
                        psH_act[hc % 4] = s_ac.n
                    w1s_free[slot] = s_pe.n
                    if s + 2 < 8:
                        sw = s_w1[slot]
                        P.wait("sync", s_pe, w1s_free[slot])
                        slab_v[s + 2] = (sw, dma(
                            "sync", w1s[:, slot * 4096:(slot + 1) * 4096]
                            .rearrange("p (dc h) -> p dc h", dc=8),
                            w1slab_src(e, s + 2), sw))
                        if tau == 0:
                            ci = len(w2chunk_vals)
                            if ci < 8:
                                src = (w2ap[e][:, ci * 4:(ci + 1) * 4, :] if hf == 0 else
                                       w2_scr[e][:, ci * 4 * D:(ci + 1) * 4 * D]
                                       .rearrange("p (hc o) -> p hc o", hc=4))
                                w2chunk_vals.append(
                                    dma("gpsimd",
                                        w2e[:, ci * 4 * D:(ci + 1) * 4 * D]
                                        .rearrange("p (hc o) -> p hc o", hc=4),
                                        src, s_w2))
                xnb_free[xslot] = s_pe.n
                v_h = s_ac.n

                P.wait("tensor", s_ac, v_h)
                P.wait("tensor", s_w2, w2chunk_vals[-1])
                P.wait("tensor", s_b2[bslot], v_b2)
                if e == 0:
                    P.wait("vector", acc_sems[tau % 2], acc_vals[(hf, tau)])
                for j in range(4):
                    gl = tau * 4 + j
                    G = hf * NGH + gl
                    pj = (psY[0], psY[1]) if j % 2 == 0 else (psY[2], psY[3])
                    P.wait("tensor", s_dv, psY_dve[j % 2])
                    for hi in range(2):
                        bank = pj[hi]
                        for hc in range(32):
                            mm(bank[:, 0:512],
                               hbuf[:, hc * 512 + j * 128: hc * 512 + (j + 1) * 128],
                               w2e[:, hc * D + hi * 512: hc * D + (hi + 1) * 512],
                               hc == 0, False)
                        mm(bank[:, 0:512], ones_b[:, :],
                           b2r[:, bslot * D + hi * 512: bslot * D + (hi + 1) * 512],
                           False, True, inc=1)
                    P.wait("vector", s_pe, s_pe.n)
                    for hi in range(2):
                        bank = pj[hi]
                        dve("scalar_tensor_tensor",
                            acc[:, gl * D + hi * 512: gl * D + (hi + 1) * 512],
                            bank[:, 0:512],
                            c_tm[:, G * 8 + e:G * 8 + e + 1],
                            acc[:, gl * D + hi * 512: gl * D + (hi + 1) * 512],
                            ALU.mult, ALU.add)
                    psY_dve[j % 2] = s_dv.n
                if e == E - 1:
                    P.wait("sync", s_dv, s_dv.n)
                    out_vals[(hf, tau)] = dma(
                        "sync",
                        y_tau[hf, tau],
                        acc[:, :].rearrange("p (ta g f) -> p ta g f",
                                            ta=NTAU, g=4)[:, tau],
                        out_sems[tau % 2])
                hused = s_pe.n
            w2_free = s_pe.n
            b2_free[bslot] = s_pe.n
            if hf == 0:
                P.wait("sync", s_w2, w2chunk_vals[-1])
                w2_scrw = dma("sync", w2_scr[e], w2e[:, :], s_w2w)
    b_scope.close()
    for ta in range(NTAU):
        P.wait("sync", out_sems[ta % 2], out_vals[(1, ta)])

    with nc.Block() as block:
        block.tensor(lambda e: [f(e) for f in P.ops["tensor"]])
        block.vector(lambda e: [f(e) for f in P.ops["vector"]])
        block.scalar(lambda e: [f(e) for f in P.ops["scalar"]])
        block.gpsimd(lambda e: [f(e) for f in P.ops["gpsimd"]])
        block.sync(lambda e: [f(e) for f in P.ops["sync"]])
    ctx.close()
    return nc


def kernel(**inputs):
    NT = 4096
    nc = build(NT)
    x = np.ascontiguousarray(np.asarray(inputs["L0_attn_x"], dtype=np.float32))
    B, L, _ = x.shape
    common = {
        "Wq": inputs["L0_attn_Wq"], "bq": inputs["L0_attn_bq"],
        "Wk": inputs["L0_attn_Wk"], "bk": inputs["L0_attn_bk"],
        "Wv": inputs["L0_attn_Wv"], "bv": inputs["L0_attn_bv"],
        "Wq2": inputs["L0_attn_Wq2"], "bq2": inputs["L0_attn_bq2"],
        "Wk2": inputs["L0_attn_Wk2"], "bk2": inputs["L0_attn_bk2"],
        "Wv2": inputs["L0_attn_Wv2"], "bv2": inputs["L0_attn_bv2"],
        "Wo": inputs["L0_attn_Wo"], "bo": inputs["L0_attn_bo"],
        "gamma": inputs["L0_attn_gamma"], "beta": inputs["L0_attn_beta"],
        "Wr": inputs["L0_ffn_Wr"], "W1": inputs["L0_ffn_W1"],
        "b1": inputs["L0_ffn_b1"], "W2": inputs["L0_ffn_W2"],
        "b2": inputs["L0_ffn_b2"],
    }
    common = {k: np.ascontiguousarray(np.asarray(v, dtype=np.float32))
              for k, v in common.items()}
    common["id_bf"] = np.eye(128).astype(ml_dtypes.bfloat16)
    common["id_f"] = np.eye(128).astype(np.float32)
    common["ones_f"] = np.ones((1, 128), dtype=np.float32)
    in_maps = []
    for c in range(8):
        b, hf = c // 2, c % 2
        m = dict(common)
        m["x"] = np.ascontiguousarray(x[b, hf * NT:(hf + 1) * NT, :])
        in_maps.append(m)
    res = run_bass_kernel_spmd(nc, in_maps, core_ids=list(range(8)))
    out = np.empty((B, L, D), dtype=np.float32)
    for c in range(8):
        b, hf = c // 2, c % 2
        out[b, hf * NT:(hf + 1) * NT, :] = np.asarray(res.results[c]["y"])
    return out


if __name__ == "__main__":
    import sys
    nc = build(1024 if "--small" in sys.argv else 4096)
    print("built OK; instructions:",
          sum(len(b.instructions) for f in nc.m.functions for b in f.blocks)
          if hasattr(nc, "m") else "?")


# version 08
# speedup vs baseline: 1.0866x; 1.0866x over previous
import numpy as np
import ml_dtypes
import concourse.bass as bass
import concourse.mybir as mybir
from concourse.bass_utils import run_bass_kernel_spmd
from contextlib import ExitStack

F32 = mybir.dt.float32
BF16 = mybir.dt.bfloat16
AF = mybir.ActivationFunctionType
ALU = mybir.AluOpType
AX = mybir.AxisListType
f32 = np.float32

D = 1024
NH = 16
DH = 64
E = 8
HID = 4096
EPS_LN1 = 1e-5
EPS_LN2 = 1e-3


class Sem:
    def __init__(self, handle):
        self.h = handle
        self.n = 0


class Plan:
    def __init__(self):
        self.ops = {"tensor": [], "vector": [], "scalar": [], "gpsimd": [], "sync": []}

    def em(self, eng, f):
        self.ops[eng].append(f)

    def wait(self, eng, sem, val):
        if val > 0:
            self.ops[eng].append(lambda e, s=sem.h, v=val: e.wait_ge(s, v))


def build(NT=4096, stages="acb", climit=99):
    NTILES = NT // 512
    NG = NT // 128
    NGH = NG // 2
    NTAU = NT // 1024

    nc = bass.Bass("TRN2", target_bir_lowering=False)

    ap_x = nc.dram_tensor("x", [NT, D], F32, kind="ExternalInput").ap()
    w_hbm = {n: nc.dram_tensor(n, [D, D], F32, kind="ExternalInput").ap()
             for n in ["Wq", "Wk", "Wv", "Wq2", "Wk2", "Wv2", "Wo"]}
    b_hbm = {n: nc.dram_tensor(n, [D], F32, kind="ExternalInput").ap()
             for n in ["bq", "bk", "bv", "bq2", "bk2", "bv2", "bo", "gamma", "beta"]}
    ap_Wr = nc.dram_tensor("Wr", [E, D], F32, kind="ExternalInput").ap()
    ap_W1 = nc.dram_tensor("W1", [E, D, HID], F32, kind="ExternalInput").ap()
    ap_b1 = nc.dram_tensor("b1", [E, HID], F32, kind="ExternalInput").ap()
    ap_W2 = nc.dram_tensor("W2", [E, HID, D], F32, kind="ExternalInput").ap()
    ap_b2 = nc.dram_tensor("b2", [E, D], F32, kind="ExternalInput").ap()
    ap_idbf = nc.dram_tensor("id_bf", [128, 128], BF16, kind="ExternalInput").ap()
    ap_idf = nc.dram_tensor("id_f", [128, 128], F32, kind="ExternalInput").ap()
    ap_ones = nc.dram_tensor("ones_f", [1, 128], F32, kind="ExternalInput").ap()
    ap_y = nc.dram_tensor("y", [NT, D], F32, kind="ExternalOutput").ap()

    xn_scr = nc.dram_tensor("xn_scr", [NTILES, 128, 4096], BF16).ap()
    yp_scr = nc.dram_tensor("yp_scr", [NG, 128, D], F32).ap()
    w1_scr = nc.dram_tensor("w1_scr", [E, 8, 128, 4096], BF16).ap()
    w2_scr = nc.dram_tensor("w2_scr", [E, 128, 32 * D], BF16).ap()
    cc_in = nc.dram_tensor("cc_in", [128, 512], F32).ap()
    cc_out = nc.dram_tensor("cc_out", [128, 512], F32).ap()

    P = Plan()
    ctx = ExitStack()

    def sem(name):
        return Sem(ctx.enter_context(nc.semaphore(name)))

    s_pe = sem("s_pe")
    s_dv = sem("s_dv")
    s_ac = sem("s_ac")
    s_cc = sem("s_cc")
    s_ld = sem("s_ld")
    s_ldg = sem("s_ldg")
    s_x = [sem("s_x0"), sem("s_x1")]
    s_xnw = [sem("s_xnw0"), sem("s_xnw1")]
    s_xnr = [sem("s_xnr0"), sem("s_xnr1")]
    s_yp = sem("s_yp")
    s_acc = sem("s_acc")
    s_acc1 = sem("s_acc1")
    s_out = sem("s_out")
    s_out1 = sem("s_out1")
    s_w1 = [sem("s_w1a"), sem("s_w1b")]
    s_w1g = [sem("s_w1ga"), sem("s_w1gb")]
    s_w1w = [sem("s_w1wa"), sem("s_w1wb")]
    s_w2 = sem("s_w2")
    s_w2w = sem("s_w2w")
    s_b2 = [sem("s_b2a"), sem("s_b2b")]
    s_wc = [sem("s_wca"), sem("s_wcb")]
    s_wcw = [sem("s_wcwa"), sem("s_wcwb")]
    DMA_SEMS = [s_ld, s_ldg] + s_x + s_xnw + s_xnr + \
        [s_yp, s_acc, s_acc1, s_out, s_out1] + \
        s_w1 + s_w1g + s_w1w + [s_w2, s_w2w] + s_b2 + s_wc + s_wcw

    def wait_ld(eng):
        P.wait(eng, s_ld, s_ld.n)
        P.wait(eng, s_ldg, s_ldg.n)

    def barrier(engines=("tensor", "vector", "scalar", "sync", "gpsimd")):
        for eng in engines:
            for sm in [s_pe, s_dv, s_ac] + DMA_SEMS:
                P.wait(eng, sm, sm.n)

    def mm(out_ap, lhsT, rhs, start, stop, inc=0):
        def f(t, o=out_ap, l=lhsT, r=rhs, st=start, sp=stop, k=inc):
            i = t.matmul(o, l, r, start=st, stop=sp)
            if k:
                i.then_inc(s_pe.h, k)
        P.em("tensor", f)
        if inc:
            s_pe.n += inc
        return s_pe.n

    def dma(q, out_ap, in_ap, sm):
        def f(e, o=out_ap, i=in_ap, s=sm.h):
            e.dma_start(out=o, in_=i).then_inc(s, 16)
        P.em(q, f)
        sm.n += 16
        return sm.n

    def act(out_ap, in_ap, func, bias=0.0, scale=1.0, accum=None):
        def f(e, o=out_ap, i=in_ap, fu=func, b=bias, s=scale, a=accum):
            kw = dict(bias=b if not isinstance(b, float) else f32(b),
                      scale=s if not isinstance(s, float) else f32(s))
            if a is not None:
                kw["accum_out"] = a
            e.activation(o, i, fu, **kw).then_inc(s_ac.h, 1)
        P.em("scalar", f)
        s_ac.n += 1
        return s_ac.n

    def dve(method, *args, **kw):
        def f(e, m=method, a=args, k=kw):
            getattr(e, m)(*a, **k).then_inc(s_dv.h, 1)
        P.em("vector", f)
        s_dv.n += 1
        return s_dv.n

    id_bf = ctx.enter_context(nc.sbuf_tensor("sb_id_bf", [128, 128], BF16))
    ones_f = ctx.enter_context(nc.sbuf_tensor("sb_ones_f", [1, 128], F32))
    ones_b = ctx.enter_context(nc.sbuf_tensor("sb_ones_b", [1, 128], BF16))
    c_tm = ctx.enter_context(nc.sbuf_tensor("c_tm", [128, NG * 8], F32))
    b_fm = ctx.enter_context(nc.sbuf_tensor("b_fm", [128, 32], F32))
    b1_fm = ctx.enter_context(nc.sbuf_tensor("sb_b1_fm", [128, E * 32], F32))
    stag = ctx.enter_context(nc.sbuf_tensor("stag", [32, 128], F32))

    ac_scope = ExitStack()
    qks = ac_scope.enter_context(nc.sbuf_tensor("qks", [128, NG * 16], F32))
    mrstd = ac_scope.enter_context(nc.sbuf_tensor("mrstd", [128, NG * 2], F32))
    S_sb = ac_scope.enter_context(nc.sbuf_tensor("S_sb", [128, 512], F32))
    S_tmp = ac_scope.enter_context(nc.sbuf_tensor("S_tmp", [64, 1024], F32))
    cstg = ac_scope.enter_context(nc.sbuf_tensor("cstg", [128, 2 * 4096], BF16))

    a_scope = ExitStack()

    def a_sb(name, shape, dtype=F32):
        return a_scope.enter_context(nc.sbuf_tensor(name, shape, dtype))

    def a_ps(name):
        return a_scope.enter_context(nc.psum_tensor(name, [128, 512], F32))

    wq = a_sb("wq", [128, 8 * D], BF16)
    wq2 = a_sb("wq2", [128, 8 * D], BF16)
    wk = a_sb("wk", [128, 8 * D], BF16)
    wk2 = a_sb("wk2", [128, 8 * D], BF16)
    x_sb = a_sb("x_sb", [128, 2 * 4096], F32)
    xn_tm = a_sb("xn_tm", [128, 4096], BF16)
    xn_fm = a_sb("xn_fm", [128, 2 * 4096], BF16)
    q1f = a_sb("q1f", [128, 8 * 512], BF16)
    phq = a_sb("phq", [128, 4096], BF16)
    phk = a_sb("phk", [128, 4096], BF16)
    s1e = a_sb("s1e", [128, 2 * 512], F32)
    s2e = a_sb("s2e", [128, 2 * 512], F32)
    qtmp = a_sb("qtmp", [128, 2 * 1024], F32)
    id_f = a_sb("sb_id_f", [128, 128], F32)
    wr_bf = a_sb("wr_bf", [8, 1024], BF16)
    wrfm = a_sb("wrfm", [128, 64], BF16)
    bq2r = a_sb("bq2r", [1, D], BF16)
    bk2r = a_sb("bk2r", [1, D], BF16)
    absc = a_sb("absc", [128, 48], F32)
    lntmp = a_sb("lntmp", [128, 4], F32)
    r_ex = a_sb("r_ex", [128, 32], F32)
    r_sc = a_sb("r_sc", [128, 16], F32)
    r_t1 = a_sb("r_t1", [128, 8], F32)
    r_t2 = a_sb("r_t2", [128, 8], F32)

    psS0 = a_ps("psS0")
    psS1 = a_ps("psS1")
    psA0, psA1, psA2, psA3 = a_ps("psA0"), a_ps("psA1"), a_ps("psA2"), a_ps("psA3")
    psB0, psB1 = a_ps("psB0"), a_ps("psB1")

    dma("sync", id_bf[:, :], ap_idbf, s_ld)
    dma("sync", id_f[:, :], ap_idf, s_ld)
    dma("sync", ones_f[:, :], ap_ones, s_ld)
    dma("gpsimd", bq2r[:, :], b_hbm["bq2"].rearrange("(o f) -> o f", o=1), s_ldg)
    dma("gpsimd", bk2r[:, :], b_hbm["bk2"].rearrange("(o f) -> o f", o=1), s_ldg)
    dma("gpsimd", ones_b[:, :], ap_ones, s_ldg)
    for w, dst in [("Wq", wq), ("Wq2", wq2), ("Wk", wk), ("Wk2", wk2)]:
        dma("gpsimd", dst[:, :].rearrange("p (kc o) -> p kc o", kc=8),
            w_hbm[w].rearrange("(kc p) o -> p kc o", p=128), s_ldg)
    dma("gpsimd", wr_bf[:, :], ap_Wr, s_ldg)

    stag_free = 0
    for j, bn in enumerate(["bq", "bk", "bv", "bv2"]):
        P.wait("sync", s_pe, stag_free)
        dma("sync", stag[0:8, :], b_hbm[bn].rearrange("(fc p) -> fc p", p=128), s_ld)
        P.wait("tensor", s_ld, s_ld.n)
        P.wait("tensor", s_dv, s_dv.n)
        stag_free = mm(psA0[0:128, 0:8], stag[0:8, :], id_f[0:8, 0:8], True, True, inc=1)
        P.wait("vector", s_pe, s_pe.n)
        dve("tensor_copy", b_fm[:, j * 8:(j + 1) * 8], psA0[0:128, 0:8])
    for e in range(E):
        P.wait("sync", s_pe, stag_free)
        dma("sync", stag[0:32, :], ap_b1[e].rearrange("(hc p) -> hc p", p=128), s_ld)
        P.wait("tensor", s_ld, s_ld.n)
        P.wait("tensor", s_dv, s_dv.n)
        stag_free = mm(psA1[0:128, 0:32], stag[0:32, :], id_f[0:32, 0:32], True, True, inc=1)
        P.wait("vector", s_pe, s_pe.n)
        dve("tensor_copy", b1_fm[:, e * 32:(e + 1) * 32], psA1[0:128, 0:32])
    wait_ld("tensor")
    for dc in range(8):
        mm(psA2[0:128, dc * 8:(dc + 1) * 8], wr_bf[0:8, dc * 128:(dc + 1) * 128],
           id_bf[0:8, 0:8], True, True, inc=(1 if dc == 7 else 0))
    P.wait("vector", s_pe, s_pe.n)
    dve("tensor_copy", wrfm[:, :], psA2[0:128, 0:64])
    v_pre_dv = s_dv.n

    a_xv = {}
    a_pre_dv = {}

    def a_load(t):
        if t >= NTILES:
            return
        sl_ = t % 2
        P.wait("sync", s_dv, xload_done.get(t - 2, 0))
        a_xv[t] = dma("sync",
                      x_sb[:, sl_ * 4096:(sl_ + 1) * 4096]
                      .rearrange("p (g f) -> p g f", g=4),
                      ap_x.rearrange("(t g p) f -> t p g f", p=128, g=4)[t], s_x[sl_])

    def a_pre(t, xnT_prev):
        if t >= NTILES:
            return
        sl_ = t % 2
        P.wait("vector", s_x[sl_], a_xv[t])
        for g in range(4):
            xg = x_sb[:, sl_ * 4096 + g * 1024: sl_ * 4096 + (g + 1) * 1024]
            dve("bn_stats", absc[:, g * 12:g * 12 + 6], xg[:, 0:512])
            dve("bn_stats", absc[:, g * 12 + 6:g * 12 + 12], xg[:, 512:1024])
        P.wait("vector", s_dv, s_dv.n)
        for g in range(4):
            G = t * 4 + g
            dve("bn_aggr", mrstd[:, G * 2:G * 2 + 2], absc[:, g * 12:g * 12 + 12])
        vagg = s_dv.n
        P.wait("scalar", s_dv, vagg)
        var_v = mrstd[:, (t * 4) * 2 + 1:(t * 4 + 4) * 2:2]
        act(lntmp[:, 0:4], var_v, AF.Ln, bias=EPS_LN1)
        P.wait("scalar", s_ac, s_ac.n)
        act(var_v, lntmp[:, 0:4], AF.Exp, scale=-0.5)
        P.wait("vector", s_ac, s_ac.n)
        P.wait("vector", s_pe, xnT_prev)
        for g in range(4):
            G = t * 4 + g
            dve("tensor_scalar", xn_tm[:, g * 1024:(g + 1) * 1024],
                x_sb[:, sl_ * 4096 + g * 1024: sl_ * 4096 + (g + 1) * 1024],
                mrstd[:, G * 2:G * 2 + 1], mrstd[:, G * 2 + 1:G * 2 + 2],
                ALU.subtract, ALU.mult)
        xload_done[t] = s_dv.n
        a_pre_dv[t] = s_dv.n

    w1ap_pre = ap_W1.rearrange("e (dc p) h -> e p dc h", p=128)
    conv_jobs = [(e, sl_) for e in range(E) for sl_ in range(8)]
    conv_state = {"rd": 0, "wr": 0, "rd_v": {}, "wr_v": {0: 0, 1: 0}}

    def conv_read():
        i = conv_state["rd"]
        if i >= len(conv_jobs):
            return
        e_, s_ = conv_jobs[i]
        slot = i % 2
        P.wait("gpsimd", s_wcw[slot], conv_state["wr_v"][slot])
        conv_state["rd_v"][i] = dma(
            "gpsimd", cstg[:, slot * 4096:(slot + 1) * 4096]
            .rearrange("p (dc h) -> p dc h", dc=8),
            w1ap_pre[e_][:, :, s_ * 512:(s_ + 1) * 512], s_wc[slot])
        conv_state["rd"] = i + 1

    def conv_write():
        i = conv_state["wr"]
        if i >= len(conv_jobs) or i >= conv_state["rd"]:
            return
        e_, s_ = conv_jobs[i]
        slot = i % 2
        P.wait("sync", s_wc[slot], conv_state["rd_v"][i])
        conv_state["wr_v"][slot] = dma(
            "sync", w1_scr[e_, s_].rearrange("p (dc h) -> p dc h", dc=8),
            cstg[:, slot * 4096:(slot + 1) * 4096]
            .rearrange("p (dc h) -> p dc h", dc=8), s_wcw[slot])
        conv_state["wr"] = i + 1

    def conv_step(n):
        for _ in range(n):
            conv_write()
            conv_read()

    conv_read()
    conv_read()

    xload_done = {}
    xnfm_free = {}
    xnfm_dma = {}
    q2_evac = {}
    bank_act = {}
    bank_dve = {}
    router_read = 0
    xnT_done = 0
    s_started = [False, False]
    phq_free = 0

    xnT_state = {"done": 0, "v_xnfm": {}}

    def a_xnT(t):
        if t >= NTILES:
            return
        sl_ = t % 2
        P.wait("tensor", s_dv, a_pre_dv[t])
        P.wait("tensor", s_pe, xnfm_free.get(t - 2, 0))
        P.wait("vector", s_xnw[sl_], xnfm_dma.get(t - 2, 0))
        for g in range(4):
            bank = psA0 if g % 2 == 0 else psA1
            P.wait("tensor", s_dv, bank_dve.get(("T", g % 2), 0))
            for fc in range(8):
                mm(bank[:, (fc % 4) * 128:(fc % 4) * 128 + 128],
                   xn_tm[:, g * 1024 + fc * 128: g * 1024 + (fc + 1) * 128],
                   id_bf[:, :], True, True, inc=(1 if fc % 4 == 3 else 0))
                if fc == 3 or fc == 7:
                    lo = 0 if fc == 3 else 4
                    P.wait("vector", s_pe, s_pe.n)
                    dve("tensor_copy",
                        xn_fm[:, sl_ * 4096:(sl_ + 1) * 4096]
                        .rearrange("p (fc tl) -> p fc tl", fc=8)
                        [:, lo:lo + 4, g * 128:(g + 1) * 128],
                        bank[:, 0:512].rearrange("p (s q) -> p s q", s=4))
                    bank_dve[("T", g % 2)] = s_dv.n
                    if fc == 3:
                        P.wait("tensor", s_dv, s_dv.n)
        xnT_state["v_xnfm"][t] = s_dv.n
        xnT_state["done"] = s_pe.n
        P.wait("gpsimd", s_dv, s_dv.n)
        xnfm_dma[t] = dma("gpsimd", xn_scr[t],
                          xn_fm[:, sl_ * 4096:(sl_ + 1) * 4096], s_xnw[sl_])

    a_load(0)
    a_pre(0, 0)
    a_xnT(0)
    a_load(1)
    for t in range(NTILES):
        sl = t % 2
        v_xnfm = xnT_state["v_xnfm"][t]

        a_pre(t + 1, xnT_state["done"])
        a_load(t + 2)

        conv_step(64 // (2 * NTILES) if NTILES else 0)

        if t == 0:
            wait_ld("tensor")
        for kind in ("q", "k"):
            w1_, w2_, bcol, brow_ = ((wq, wq2, 0, bq2r) if kind == "q"
                                     else (wk, wk2, 1, bk2r))
            ph = phq if kind == "q" else phk
            P.wait("tensor", s_dv, max([q2_evac.get((a, b), 0)
                                        for a in (0, 1) for b in (0, 1)] + [v_xnfm]))
            for fc in range(8):
                bank = psA2 if fc % 2 == 0 else psA3
                P.wait("tensor", s_ac, bank_act.get(("P1", fc % 2), 0))
                for dc in range(8):
                    mm(bank[:, 0:512],
                       w1_[:, dc * D + fc * 128: dc * D + (fc + 1) * 128],
                       xn_fm[:, sl * 4096 + dc * 512: sl * 4096 + (dc + 1) * 512],
                       dc == 0, dc == 7, inc=(1 if dc == 7 else 0))
                P.wait("scalar", s_pe, s_pe.n)
                act(q1f[:, fc * 512:(fc + 1) * 512], bank[:, 0:512],
                    AF.Identity, bias=b_fm[:, bcol * 8 + fc: bcol * 8 + fc + 1])
                bank_act[("P1", fc % 2)] = s_ac.n
            v_q1 = s_ac.n
            P.wait("tensor", s_ac, v_q1)
            for g in range(4):
                pair = (psB0, psB1) if g % 2 == 0 else (psA2, psA3)
                for hi in range(2):
                    bank = pair[hi]
                    P.wait("tensor", s_dv, max(q2_evac.get((g % 2, hi), 0),
                                               router_read))
                    for fc in range(8):
                        mm(bank[:, 0:512],
                           q1f[:, fc * 512 + g * 128: fc * 512 + (g + 1) * 128],
                           w2_[:, fc * D + hi * 512: fc * D + (hi + 1) * 512],
                           fc == 0, False)
                    mm(bank[:, 0:512], ones_b[:, :],
                       brow_[:, hi * 512:(hi + 1) * 512], False, True, inc=1)
                    vpe = s_pe.n
                    P.wait("scalar", s_pe, vpe)
                    P.wait("scalar", s_dv, q2_evac.get(("s2", hi), 0))
                    act(s1e[:, hi * 512:(hi + 1) * 512],
                        bank[:, 0:512], AF.Relu, scale=-1.0)
                    P.wait("scalar", s_ac, s_ac.n)
                    act(s2e[:, hi * 512:(hi + 1) * 512],
                        s1e[:, hi * 512:(hi + 1) * 512], AF.Exp, scale=-1.0)
                    bank_act[("E", hi)] = s_ac.n
                    P.wait("vector", s_ac, s_ac.n)
                    P.wait("vector", s_pe, vpe)
                    dve("scalar_tensor_tensor",
                        ph[:, g * 1024 + hi * 512: g * 1024 + (hi + 1) * 512],
                        bank[:, 0:512], 0.0,
                        s2e[:, hi * 512:(hi + 1) * 512], ALU.max, ALU.add)
                    q2_evac[(g % 2, hi)] = s_dv.n
                    q2_evac[("s2", hi)] = s_dv.n
        v_phik = s_dv.n

        a_xnT(t + 1)

        P.wait("vector", s_dv, v_phik)
        qred = {}
        for g in range(4):
            G = t * 4 + g
            qs = g % 2
            P.wait("vector", s_dv, qred.get(qs, 0))
            dve("tensor_tensor", qtmp[:, qs * 1024:(qs + 1) * 1024],
                phq[:, g * 1024:(g + 1) * 1024],
                phk[:, g * 1024:(g + 1) * 1024], ALU.mult)
            P.wait("vector", s_dv, s_dv.n)
            dve("tensor_reduce", qks[:, G * 16:(G + 1) * 16],
                qtmp[:, qs * 1024:(qs + 1) * 1024]
                .rearrange("p (h k) -> p h k", k=64), AX.X, ALU.add)
            qred[qs] = s_dv.n
        P.wait("tensor", s_dv, v_phik)
        for g in range(4):
            for h in range(NH):
                bankS = psS0 if h % 2 == 0 else psS1
                c0 = (h // 2) * 64
                first = not s_started[h % 2]
                s_started[h % 2] = True
                last = (t == NTILES - 1 and g == 3 and h >= NH - 2)
                mm(bankS[0:64, c0:c0 + 64],
                   phk[:, g * 1024 + h * 64: g * 1024 + (h + 1) * 64],
                   phq[:, g * 1024 + h * 64: g * 1024 + (h + 1) * 64],
                   first, last, inc=(1 if (g == 3 and h == NH - 1) else 0))
        phq_free = s_pe.n

        P.wait("tensor", s_dv, max([q2_evac.get((a, b), 0)
                                    for a in (0, 1) for b in (0, 1)] + [router_read]))
        for g in range(4):
            for dc in range(8):
                mm(psB0[:, g * 8:(g + 1) * 8],
                   xn_fm[:, sl * 4096 + dc * 512 + g * 128:
                         sl * 4096 + dc * 512 + (g + 1) * 128],
                   wrfm[:, dc * 8:(dc + 1) * 8],
                   dc == 0, dc == 7, inc=(1 if (g == 3 and dc == 7) else 0))
        xnfm_free[t] = s_pe.n
        v_rpe = s_pe.n
        for g in range(4):
            G = t * 4 + g
            P.wait("vector", s_pe, v_rpe)
            dve("reduce_max", r_sc[:, g:g + 1], psB0[:, g * 8:(g + 1) * 8], AX.X,
                negate=True)
            P.wait("scalar", s_dv, s_dv.n)
            P.wait("scalar", s_pe, v_rpe)
            act(r_ex[:, g * 8:(g + 1) * 8], psB0[:, g * 8:(g + 1) * 8], AF.Exp,
                bias=r_sc[:, g:g + 1], accum=r_sc[:, 4 + g:5 + g])
            P.wait("vector", s_ac, s_ac.n)
            dve("reduce_max", r_sc[:, 8 + g:9 + g], r_ex[:, g * 8:(g + 1) * 8], AX.X)
            P.wait("vector", s_dv, s_dv.n)
            dve("tensor_scalar", r_t1[:, :], r_ex[:, g * 8:(g + 1) * 8],
                r_sc[:, 8 + g:9 + g], None, ALU.is_ge)
            P.wait("vector", s_dv, s_dv.n)
            dve("scalar_tensor_tensor", r_t2[:, :], r_t1[:, :], -4.0,
                r_ex[:, g * 8:(g + 1) * 8], ALU.mult, ALU.add)
            P.wait("vector", s_dv, s_dv.n)
            dve("reduce_max", r_sc[:, 12 + g:13 + g], r_t2[:, :], AX.X)
            P.wait("vector", s_dv, s_dv.n)
            dve("tensor_scalar", r_t1[:, :], r_ex[:, g * 8:(g + 1) * 8],
                r_sc[:, 12 + g:13 + g], None, ALU.is_ge)
            dve("reciprocal", r_sc[:, 4 + g:5 + g], r_sc[:, 4 + g:5 + g])
            P.wait("vector", s_dv, s_dv.n)
            dve("tensor_tensor", r_t2[:, :], r_ex[:, g * 8:(g + 1) * 8],
                r_t1[:, :], ALU.mult)
            P.wait("vector", s_dv, s_dv.n)
            dve("tensor_scalar", c_tm[:, G * 8:(G + 1) * 8], r_t2[:, :],
                r_sc[:, 4 + g:5 + g], None, ALU.mult)
        router_read = s_dv.n

    P.wait("vector", s_pe, s_pe.n)
    dve("tensor_copy", S_tmp[:, 0:512], psS0[0:64, 0:512])
    dve("tensor_copy", S_tmp[:, 512:1024], psS1[0:64, 0:512])
    P.wait("sync", s_dv, s_dv.n)
    dma("sync", cc_in[0:64, :], S_tmp[:, 0:512], s_ld)
    dma("sync", cc_in[64:128, :], S_tmp[:, 512:1024], s_ld)
    P.wait("gpsimd", s_ld, s_ld.n)

    def f_cc(e):
        e.collective_compute(
            "AllReduce", ALU.add,
            replica_groups=[[0, 1], [2, 3], [4, 5], [6, 7]],
            ins=[cc_in], outs=[cc_out]).then_inc(s_cc.h, 1)
    P.em("gpsimd", f_cc)
    s_cc.n += 1
    P.wait("gpsimd", s_cc, 1)
    v_sfull = dma("gpsimd", S_sb[:, :], cc_out, s_ldg)

    a_scope.close()

    if stages == "a":
        barrier(("sync",))
        v_fin = dma("sync", ap_y, ap_x, s_out)
        P.wait("sync", s_out, v_fin)
        with nc.Block() as block:
            block.tensor(lambda e: [f(e) for f in P.ops["tensor"]])
            block.vector(lambda e: [f(e) for f in P.ops["vector"]])
            block.scalar(lambda e: [f(e) for f in P.ops["scalar"]])
            block.gpsimd(lambda e: [f(e) for f in P.ops["gpsimd"]])
            block.sync(lambda e: [f(e) for f in P.ops["sync"]])
        ac_scope.close()
        ctx.close()
        return nc

    c_scope = ExitStack()

    def c_sb(name, shape, dtype=F32):
        return c_scope.enter_context(nc.sbuf_tensor(name, shape, dtype))

    def c_ps(name):
        return c_scope.enter_context(nc.psum_tensor(name, [128, 512], F32))

    wo = c_sb("wo", [128, 8 * D], BF16)
    wv = c_sb("wv", [128, 8 * D], BF16)
    wv2 = c_sb("wv2", [128, 8 * D], BF16)
    xc = c_sb("xc", [128, 4096], F32)
    xnf_c = c_sb("xnf_c", [128, 4096], BF16)
    xn32 = c_sb("xn32", [128, 4096], F32)
    v1f = c_sb("v1f", [128, 8 * 512], BF16)
    vf = c_sb("vf", [128, 8 * 512], BF16)
    atf = c_sb("atf", [128, 4096], BF16)
    y_c = c_sb("y_c", [128, 4096], F32)
    z_c = c_sb("z_c", [128, 4096], F32)
    gbc = c_sb("gbc", [128, D], F32)
    bbc = c_sb("bbc", [128, D], F32)
    grow = c_sb("grow", [1, D], F32)
    brow = c_sb("brow", [1, D], F32)
    borow = c_sb("borow", [1, D], BF16)
    S_blk = c_sb("S_blk", [128, 1024], BF16)
    qks_inv = c_sb("qks_inv", [128, NG * 16], F32)
    cst = c_sb("cst", [128, 64], F32)
    clnt = c_sb("clnt", [128, 4], F32)

    psV0, psV1 = c_ps("psV0"), c_ps("psV1")
    psO0, psO1 = c_ps("psO0"), c_ps("psO1")
    psT0, psT1 = c_ps("psT0"), c_ps("psT1")
    psW0, psW1 = c_ps("psW0"), c_ps("psW1")

    barrier()

    for w, dst in [("Wv", wv), ("Wv2", wv2), ("Wo", wo)]:
        dma("gpsimd", dst[:, :].rearrange("p (kc o) -> p kc o", kc=8),
            w_hbm[w].rearrange("(kc p) o -> p kc o", p=128), s_ldg)
    v_cw = s_ldg.n
    dma("sync", grow[:, :], b_hbm["gamma"].rearrange("(o f) -> o f", o=1), s_ld)
    dma("sync", brow[:, :], b_hbm["beta"].rearrange("(o f) -> o f", o=1), s_ld)
    dma("gpsimd", borow[:, :], b_hbm["bo"].rearrange("(o f) -> o f", o=1), s_ldg)
    P.wait("tensor", s_ld, s_ld.n)
    for (row, dst) in [(grow, gbc), (brow, bbc)]:
        for hi in range(2):
            P.wait("tensor", s_dv, s_dv.n)
            mm(psW0[:, 0:512], ones_f[:, :], row[:, hi * 512:(hi + 1) * 512],
               True, True, inc=1)
            P.wait("vector", s_pe, s_pe.n)
            dve("tensor_copy", dst[:, hi * 512:(hi + 1) * 512], psW0[:, 0:512])
    P.wait("vector", s_ldg, v_sfull)
    dve("memset", S_blk[:, :], 0.0)
    P.wait("vector", s_dv, s_dv.n)
    dve("tensor_copy",
        S_blk[0:64, :].rearrange("p (j c) -> p j c", c=128)[:, :, 0:64],
        S_sb[0:64, :].rearrange("p (j c) -> p j c", c=64))
    dve("tensor_copy",
        S_blk[64:128, :].rearrange("p (j c) -> p j c", c=128)[:, :, 64:128],
        S_sb[64:128, :].rearrange("p (j c) -> p j c", c=64))
    dve("reciprocal", qks_inv[:, :], qks[:, :])
    v_cpre = s_dv.n

    c_xused = {}
    c_ypdma = {}
    cb_act = {}
    cb_dve = {}
    xnf_used = 0

    for t in range(NTILES):
        sl = 0
        P.wait("sync", s_dv, c_xused.get(t - 1, 0))
        v_cx = dma("sync",
                   xc[:, :].rearrange("p (g f) -> p g f", g=4),
                   ap_x.rearrange("(t g p) f -> t p g f", p=128, g=4)[t], s_x[0])
        P.wait("sync", s_pe, xnf_used)
        v_cxn = dma("sync", xnf_c[:, :], xn_scr[t], s_xnr[0])

        P.wait("vector", s_x[0], v_cx)
        P.wait("vector", s_yp, c_ypdma.get(t - 1, 0))
        if t == 0:
            P.wait("vector", s_ld, s_ld.n)
        for g in range(4):
            G = t * 4 + g
            dve("tensor_scalar", xn32[:, g * 1024:(g + 1) * 1024],
                xc[:, g * 1024:(g + 1) * 1024],
                mrstd[:, G * 2:G * 2 + 1], mrstd[:, G * 2 + 1:G * 2 + 2],
                ALU.subtract, ALU.mult)
            dve("tensor_tensor", z_c[:, g * 1024:(g + 1) * 1024],
                xc[:, g * 1024:(g + 1) * 1024],
                bbc[:, :], ALU.add)
        v_xn32 = s_dv.n
        c_xused[t] = s_dv.n

        conv_step(64 // (2 * NTILES) if NTILES else 0)

        P.wait("tensor", s_xnr[0], v_cxn)
        P.wait("tensor", s_ldg, s_ldg.n)
        for fc in range(8):
            bank = psV0 if fc % 2 == 0 else psV1
            P.wait("tensor", s_ac, cb_act.get(("V1", fc % 2), 0))
            for dc in range(8):
                mm(bank[:, 0:512],
                   wv[:, dc * D + fc * 128: dc * D + (fc + 1) * 128],
                   xnf_c[:, dc * 512:(dc + 1) * 512],
                   dc == 0, dc == 7, inc=(1 if dc == 7 else 0))
            P.wait("scalar", s_pe, s_pe.n)
            act(v1f[:, fc * 512:(fc + 1) * 512], bank[:, 0:512],
                AF.Identity, bias=b_fm[:, 16 + fc:17 + fc])
            cb_act[("V1", fc % 2)] = s_ac.n
        v_v1 = s_ac.n
        xnf_used = s_pe.n
        P.wait("tensor", s_ac, v_v1)
        for fc in range(8):
            bank = psV0 if fc % 2 == 0 else psV1
            P.wait("tensor", s_ac, cb_act.get(("V2", fc % 2), 0))
            for dc in range(8):
                mm(bank[:, 0:512],
                   wv2[:, dc * D + fc * 128: dc * D + (fc + 1) * 128],
                   v1f[:, dc * 512:(dc + 1) * 512],
                   dc == 0, dc == 7, inc=(1 if dc == 7 else 0))
            P.wait("scalar", s_pe, s_pe.n)
            act(vf[:, fc * 512:(fc + 1) * 512], bank[:, 0:512],
                AF.Identity, bias=b_fm[:, 24 + fc:25 + fc])
            cb_act[("V2", fc % 2)] = s_ac.n
        v_v2 = s_ac.n
        if climit < 1:
            continue

        P.wait("tensor", s_ac, v_v2)
        if t == 0:
            P.wait("tensor", s_dv, v_cpre)
        for g in range(4):
            pair = (psO0, psO1) if g % 2 == 0 else (psT0, psT1)
            G = t * 4 + g
            P.wait("tensor", s_dv, cb_dve.get(("O", g % 2), 0))
            for j in range(8):
                bank = pair[j // 4]
                mm(bank[:, (j % 4) * 128:(j % 4) * 128 + 128],
                   vf[:, j * 512 + g * 128: j * 512 + (g + 1) * 128],
                   S_blk[:, j * 128:(j + 1) * 128],
                   True, True, inc=(1 if j == 7 else 0))
            P.wait("vector", s_pe, s_pe.n)
            for h in range(NH):
                j = h // 2
                bank = pair[j // 4]
                dve("tensor_scalar",
                    v1f[:, g * 1024 + h * 64: g * 1024 + (h + 1) * 64],
                    bank[:, (j % 4) * 128 + (h % 2) * 64:
                         (j % 4) * 128 + (h % 2) * 64 + 64],
                    qks_inv[:, G * 16 + h:G * 16 + h + 1], None, ALU.mult)
            cb_dve[("O", g % 2)] = s_dv.n
        v_atm = s_dv.n
        if climit < 2:
            continue

        P.wait("tensor", s_dv, v_atm)
        for g in range(4):
            bank = psT0 if g % 2 == 0 else psT1
            P.wait("tensor", s_dv, cb_dve.get(("AT", g % 2), 0))
            for fc in range(8):
                mm(bank[:, (fc % 4) * 128:(fc % 4) * 128 + 128],
                   v1f[:, g * 1024 + fc * 128: g * 1024 + (fc + 1) * 128],
                   id_bf[:, :], True, True, inc=(1 if fc % 4 == 3 else 0))
                if fc == 3 or fc == 7:
                    lo = 0 if fc == 3 else 4
                    P.wait("vector", s_pe, s_pe.n)
                    dve("tensor_copy",
                        atf[:, :].rearrange("p (fc tl) -> p fc tl", fc=8)
                        [:, lo:lo + 4, g * 128:(g + 1) * 128],
                        bank[:, 0:512].rearrange("p (s q) -> p s q", s=4))
                    cb_dve[("AT", g % 2)] = s_dv.n
                    if fc == 3:
                        P.wait("tensor", s_dv, s_dv.n)
        v_atf = s_dv.n
        if climit < 3:
            continue

        if t == 0:
            P.wait("tensor", s_ldg, s_ldg.n)
        P.wait("tensor", s_dv, v_atf)
        for g in range(4):
            for hi in range(2):
                bank = psW0 if hi == 0 else psW1
                P.wait("tensor", s_dv, cb_dve.get(("W", hi), 0))
                for fc in range(8):
                    mm(bank[:, 0:512],
                       atf[:, fc * 512 + g * 128: fc * 512 + (g + 1) * 128],
                       wo[:, fc * D + hi * 512: fc * D + (hi + 1) * 512],
                       fc == 0, False)
                mm(bank[:, 0:512], ones_b[:, :],
                   borow[:, hi * 512:(hi + 1) * 512], False, True, inc=1)
                P.wait("vector", s_pe, s_pe.n)
                P.wait("vector", s_dv, v_xn32)
                dve("tensor_tensor",
                    y_c[:, g * 1024 + hi * 512: g * 1024 + (hi + 1) * 512],
                    bank[:, 0:512],
                    xn32[:, g * 1024 + hi * 512: g * 1024 + (hi + 1) * 512],
                    ALU.add)
                cb_dve[("W", hi)] = s_dv.n
        if climit < 4:
            continue
        P.wait("vector", s_dv, s_dv.n)
        for g in range(4):
            yg = y_c[:, g * 1024:(g + 1) * 1024]
            dve("bn_stats", cst[:, g * 12:g * 12 + 6], yg[:, 0:512])
            dve("bn_stats", cst[:, g * 12 + 6:g * 12 + 12], yg[:, 512:1024])
        P.wait("vector", s_dv, s_dv.n)
        for g in range(4):
            dve("bn_aggr", cst[:, 48 + g * 2:48 + g * 2 + 2],
                cst[:, g * 12:g * 12 + 12])
        P.wait("scalar", s_dv, s_dv.n)
        var_v = cst[:, 49:57:2]
        act(clnt[:, 0:4], var_v, AF.Ln, bias=EPS_LN2)
        P.wait("scalar", s_ac, s_ac.n)
        act(var_v, clnt[:, 0:4], AF.Exp, scale=-0.5)
        P.wait("vector", s_ac, s_ac.n)
        for g in range(4):
            yg = y_c[:, g * 1024:(g + 1) * 1024]
            xg = xn32[:, g * 1024:(g + 1) * 1024]
            dve("tensor_scalar", xg, yg, cst[:, 48 + g * 2:48 + g * 2 + 1],
                cst[:, 49 + g * 2:50 + g * 2], ALU.subtract, ALU.mult)
            P.wait("vector", s_dv, s_dv.n)
            dve("tensor_tensor", yg, xg, gbc[:, :], ALU.mult)
            P.wait("vector", s_dv, s_dv.n)
            dve("tensor_tensor", xg, yg,
                z_c[:, g * 1024:(g + 1) * 1024], ALU.add)
        P.wait("gpsimd", s_dv, s_dv.n)
        c_ypdma[t] = dma("gpsimd",
                         yp_scr.rearrange("(t g) p f -> t p g f", g=4)[t],
                         xn32[:, :].rearrange("p (g f) -> p g f", g=4), s_yp)

    conv_step(4)
    c_scope.close()
    ac_scope.close()

    if stages == "ac":
        barrier(("sync",))
        if climit >= 4:
            v_fin = dma("sync", ap_y.rearrange("(G p) f -> G p f", p=128), yp_scr, s_out)
        else:
            v_fin = dma("sync", ap_y, ap_x, s_out)
        P.wait("sync", s_out, v_fin)
        with nc.Block() as block:
            block.tensor(lambda e: [f(e) for f in P.ops["tensor"]])
            block.vector(lambda e: [f(e) for f in P.ops["vector"]])
            block.scalar(lambda e: [f(e) for f in P.ops["scalar"]])
            block.gpsimd(lambda e: [f(e) for f in P.ops["gpsimd"]])
            block.sync(lambda e: [f(e) for f in P.ops["sync"]])
        ctx.close()
        return nc

    b_scope = ExitStack()

    def b_sb(name, shape, dtype=F32):
        return b_scope.enter_context(nc.sbuf_tensor(name, shape, dtype))

    w2e = b_sb("w2e", [128, 32 * D], BF16)
    acc = b_sb("acc", [128, NGH * D], F32)
    w1s = b_sb("w1s", [128, 2 * 4096], BF16)
    hbuf = b_sb("hbuf", [128, 32 * 512], BF16)
    xnb = b_sb("xnb", [128, 2 * 4096], BF16)
    b2r = b_sb("b2r", [1, 2 * D], BF16)

    psH = [b_scope.enter_context(nc.psum_tensor(f"psH{i}", [128, 512], F32))
           for i in range(4)]
    psY = [b_scope.enter_context(nc.psum_tensor(f"psY{i}", [128, 512], F32))
           for i in range(4)]

    barrier()

    w1ap = ap_W1.rearrange("e (dc p) h -> e p dc h", p=128)
    w2ap = ap_W2.rearrange("e (hc p) o -> e p hc o", p=128)

    w1s_free = {0: 0, 1: 0}
    w1s_scrw = {0: 0, 1: 0}
    xnb_free = {0: 0, 1: 0}
    b2_free = {0: 0, 1: 0}
    w2_free = 0
    w2_scrw = 0
    psH_act = [0] * 4
    psY_dve = [0, 0]
    hused = 0
    acc_out = 0
    acc_evac = 0

    def w1slab_src(e, s):
        return w1_scr[e, s].rearrange("p (dc h) -> p dc h", dc=8)

    y_tau = ap_y.rearrange("(hf ta g p) f -> hf ta p g f", p=128, g=4, ta=NTAU)
    yp_tau = yp_scr.rearrange("(hf ta g) p f -> hf ta p g f", g=4, ta=NTAU)
    out_sems = [s_out, s_out1]
    acc_sems = [s_acc, s_acc1]
    out_vals = {}
    acc_vals = {}
    for hf in range(2):
        for ta in range(NTAU):
            if hf == 1:
                P.wait("sync", out_sems[ta % 2], out_vals[(0, ta)])
            acc_vals[(hf, ta)] = dma(
                "sync",
                acc[:, :].rearrange("p (ta g f) -> p ta g f", ta=NTAU, g=4)[:, ta],
                yp_tau[hf, ta], acc_sems[ta % 2])

        for e in range(E):
            bslot = e % 2
            P.wait("sync", s_pe, b2_free[bslot])
            v_b2 = dma("gpsimd", b2r[:, bslot * D:(bslot + 1) * D],
                       ap_b2[e].rearrange("(o f) -> o f", o=1), s_b2[bslot])

            w2chunk_vals = []

            for tau in range(NTAU):
                t_abs = hf * NTAU + tau
                xslot = t_abs % 2
                P.wait("sync", s_pe, xnb_free[xslot])
                v_xnb = dma("sync", xnb[:, xslot * 4096:(xslot + 1) * 4096],
                            xn_scr[t_abs], s_xnr[xslot])

                first_touch = (hf == 0 and tau == 0)
                slab_v = {}
                for s in (0, 1):
                    sw = s_w1[s]
                    P.wait("sync", s_pe, w1s_free[s])
                    slab_v[s] = (sw, dma("sync", w1s[:, s * 4096:(s + 1) * 4096]
                                         .rearrange("p (dc h) -> p dc h", dc=8),
                                         w1slab_src(e, s), sw))
                    if tau == 0:
                        ci = len(w2chunk_vals)
                        if ci < 8:
                            if ci == 0:
                                P.wait("gpsimd", s_pe, w2_free)
                                P.wait("gpsimd", s_w2w, w2_scrw)
                            src = (w2ap[e][:, ci * 4:(ci + 1) * 4, :] if hf == 0 else
                                   w2_scr[e][:, ci * 4 * D:(ci + 1) * 4 * D]
                                   .rearrange("p (hc o) -> p hc o", hc=4))
                            w2chunk_vals.append(
                                dma("gpsimd",
                                    w2e[:, ci * 4 * D:(ci + 1) * 4 * D]
                                    .rearrange("p (hc o) -> p hc o", hc=4),
                                    src, s_w2))
                P.wait("tensor", s_xnr[xslot], v_xnb)
                for s in range(8):
                    slot = s % 2
                    P.wait("tensor", slab_v[s][0], slab_v[s][1])
                    for hcl in range(4):
                        hc = s * 4 + hcl
                        bank = psH[hc % 4]
                        P.wait("tensor", s_ac, psH_act[hc % 4])
                        for dc in range(8):
                            mm(bank[:, 0:512],
                               w1s[:, slot * 4096 + dc * 512 + hcl * 128:
                                   slot * 4096 + dc * 512 + (hcl + 1) * 128],
                               xnb[:, xslot * 4096 + dc * 512:
                                   xslot * 4096 + (dc + 1) * 512],
                               dc == 0, dc == 7, inc=(1 if dc == 7 else 0))
                        if hc == 0:
                            P.wait("scalar", s_pe, hused)
                        P.wait("scalar", s_pe, s_pe.n)
                        act(hbuf[:, hc * 512:(hc + 1) * 512], bank[:, 0:512],
                            AF.Gelu, bias=b1_fm[:, e * 32 + hc:e * 32 + hc + 1])
                        psH_act[hc % 4] = s_ac.n
                    w1s_free[slot] = s_pe.n
                    if s + 2 < 8:
                        sw = s_w1[slot]
                        P.wait("sync", s_pe, w1s_free[slot])
                        slab_v[s + 2] = (sw, dma(
                            "sync", w1s[:, slot * 4096:(slot + 1) * 4096]
                            .rearrange("p (dc h) -> p dc h", dc=8),
                            w1slab_src(e, s + 2), sw))
                        if tau == 0:
                            ci = len(w2chunk_vals)
                            if ci < 8:
                                src = (w2ap[e][:, ci * 4:(ci + 1) * 4, :] if hf == 0 else
                                       w2_scr[e][:, ci * 4 * D:(ci + 1) * 4 * D]
                                       .rearrange("p (hc o) -> p hc o", hc=4))
                                w2chunk_vals.append(
                                    dma("gpsimd",
                                        w2e[:, ci * 4 * D:(ci + 1) * 4 * D]
                                        .rearrange("p (hc o) -> p hc o", hc=4),
                                        src, s_w2))
                xnb_free[xslot] = s_pe.n
                v_h = s_ac.n

                P.wait("tensor", s_ac, v_h)
                P.wait("tensor", s_w2, w2chunk_vals[-1])
                P.wait("tensor", s_b2[bslot], v_b2)
                if e == 0:
                    P.wait("vector", acc_sems[tau % 2], acc_vals[(hf, tau)])
                for j in range(4):
                    gl = tau * 4 + j
                    G = hf * NGH + gl
                    pj = (psY[0], psY[1]) if j % 2 == 0 else (psY[2], psY[3])
                    P.wait("tensor", s_dv, psY_dve[j % 2])
                    for hi in range(2):
                        bank = pj[hi]
                        for hc in range(32):
                            mm(bank[:, 0:512],
                               hbuf[:, hc * 512 + j * 128: hc * 512 + (j + 1) * 128],
                               w2e[:, hc * D + hi * 512: hc * D + (hi + 1) * 512],
                               hc == 0, False)
                        mm(bank[:, 0:512], ones_b[:, :],
                           b2r[:, bslot * D + hi * 512: bslot * D + (hi + 1) * 512],
                           False, True, inc=1)
                    P.wait("vector", s_pe, s_pe.n)
                    for hi in range(2):
                        bank = pj[hi]
                        dve("scalar_tensor_tensor",
                            acc[:, gl * D + hi * 512: gl * D + (hi + 1) * 512],
                            bank[:, 0:512],
                            c_tm[:, G * 8 + e:G * 8 + e + 1],
                            acc[:, gl * D + hi * 512: gl * D + (hi + 1) * 512],
                            ALU.mult, ALU.add)
                    psY_dve[j % 2] = s_dv.n
                if e == E - 1:
                    P.wait("sync", s_dv, s_dv.n)
                    out_vals[(hf, tau)] = dma(
                        "sync",
                        y_tau[hf, tau],
                        acc[:, :].rearrange("p (ta g f) -> p ta g f",
                                            ta=NTAU, g=4)[:, tau],
                        out_sems[tau % 2])
                hused = s_pe.n
            w2_free = s_pe.n
            b2_free[bslot] = s_pe.n
            if hf == 0:
                P.wait("sync", s_w2, w2chunk_vals[-1])
                w2_scrw = dma("sync", w2_scr[e], w2e[:, :], s_w2w)
    b_scope.close()
    for ta in range(NTAU):
        P.wait("sync", out_sems[ta % 2], out_vals[(1, ta)])

    with nc.Block() as block:
        block.tensor(lambda e: [f(e) for f in P.ops["tensor"]])
        block.vector(lambda e: [f(e) for f in P.ops["vector"]])
        block.scalar(lambda e: [f(e) for f in P.ops["scalar"]])
        block.gpsimd(lambda e: [f(e) for f in P.ops["gpsimd"]])
        block.sync(lambda e: [f(e) for f in P.ops["sync"]])
    ctx.close()
    return nc


def kernel(**inputs):
    NT = 4096
    nc = build(NT)
    x = np.ascontiguousarray(np.asarray(inputs["L0_attn_x"], dtype=np.float32))
    B, L, _ = x.shape
    common = {
        "Wq": inputs["L0_attn_Wq"], "bq": inputs["L0_attn_bq"],
        "Wk": inputs["L0_attn_Wk"], "bk": inputs["L0_attn_bk"],
        "Wv": inputs["L0_attn_Wv"], "bv": inputs["L0_attn_bv"],
        "Wq2": inputs["L0_attn_Wq2"], "bq2": inputs["L0_attn_bq2"],
        "Wk2": inputs["L0_attn_Wk2"], "bk2": inputs["L0_attn_bk2"],
        "Wv2": inputs["L0_attn_Wv2"], "bv2": inputs["L0_attn_bv2"],
        "Wo": inputs["L0_attn_Wo"], "bo": inputs["L0_attn_bo"],
        "gamma": inputs["L0_attn_gamma"], "beta": inputs["L0_attn_beta"],
        "Wr": inputs["L0_ffn_Wr"], "W1": inputs["L0_ffn_W1"],
        "b1": inputs["L0_ffn_b1"], "W2": inputs["L0_ffn_W2"],
        "b2": inputs["L0_ffn_b2"],
    }
    common = {k: np.ascontiguousarray(np.asarray(v, dtype=np.float32))
              for k, v in common.items()}
    common["id_bf"] = np.eye(128).astype(ml_dtypes.bfloat16)
    common["id_f"] = np.eye(128).astype(np.float32)
    common["ones_f"] = np.ones((1, 128), dtype=np.float32)
    in_maps = []
    for c in range(8):
        b, hf = c // 2, c % 2
        m = dict(common)
        m["x"] = np.ascontiguousarray(x[b, hf * NT:(hf + 1) * NT, :])
        in_maps.append(m)
    res = run_bass_kernel_spmd(nc, in_maps, core_ids=list(range(8)))
    out = np.empty((B, L, D), dtype=np.float32)
    for c in range(8):
        b, hf = c // 2, c % 2
        out[b, hf * NT:(hf + 1) * NT, :] = np.asarray(res.results[c]["y"])
    return out


if __name__ == "__main__":
    import sys
    nc = build(1024 if "--small" in sys.argv else 4096)
    print("built OK; instructions:",
          sum(len(b.instructions) for f in nc.m.functions for b in f.blocks)
          if hasattr(nc, "m") else "?")
